# version 01
# baseline (speedup 1.0000x reference)
import numpy as np
from contextlib import ExitStack

import concourse.bass as bass
import concourse.mybir as mybir
from concourse.bass_utils import run_bass_kernel_spmd

F32 = mybir.dt.float32
BF16 = mybir.dt.bfloat16
AF = mybir.ActivationFunctionType
ALU = mybir.AluOpType
AX = mybir.AxisListType

SEQ, D = 4096, 1024
NQ, TQ, NT = 4, 1024, 8
NG = 2
NE, NHG, NHC = 10, 4, 4
NDC = 8
EXP_SHIFT = -2.0
NEGBIG = -1e30

ENGINES = ("pe", "act", "dve", "pool", "sp")


class Prog:

    def __init__(self):
        self.ops = {e: [] for e in ENGINES}
        self.cnt = {}

    def emit(self, eng, fn, waits=(), inc=None):
        waits = list(waits)
        if eng in ("act", "dve"):
            if inc is None:
                inc = (eng, 1)
            prev = self.cnt.get(eng, 0)
            if prev:
                waits.append((eng, prev))
        elif eng == "pool" and inc is not None and inc[0] == "pool":
            prev = self.cnt.get("pool", 0)
            if prev:
                waits.append(("pool", prev))
        val = None
        if inc is not None:
            s, a = inc
            self.cnt[s] = self.cnt.get(s, 0) + a
            val = self.cnt[s]
        self.ops[eng].append((fn, [(s, v) for (s, v) in waits if v > 0], inc))
        return val

    def wait_only(self, eng, waits):
        self.ops[eng].append((None, [(s, v) for (s, v) in waits if v > 0], None))

    def now(self, sem):
        return self.cnt.get(sem, 0)


def build_nc(phases="lraf"):
    alvl = 5
    import re as _re
    _m = _re.search(r"a(\d)", phases)
    if _m:
        alvl = int(_m.group(1))
    nc = bass.Bass("TRN2", target_bir_lowering=False)

    Xh = nc.dram_tensor("x", [SEQ, D], F32, kind="ExternalInput").ap()
    WAh = nc.dram_tensor("wa", [3, 128, 8192], F32, kind="ExternalInput").ap()
    WOh = nc.dram_tensor("wo", [128, 8192], F32, kind="ExternalInput").ap()
    W13h = nc.dram_tensor("wf13", [NE, NHG, 128, 8192], F32, kind="ExternalInput").ap()
    W2h = nc.dram_tensor("wf2", [NE, NHG, 128, 4096], F32, kind="ExternalInput").ap()
    RWh = nc.dram_tensor("rwt", [128, 64], F32, kind="ExternalInput").ap()
    BBh = nc.dram_tensor("biasb", [128, 8], F32, kind="ExternalInput").ap()
    MKh = nc.dram_tensor("masks", [128, 512], F32, kind="ExternalInput").ap()
    IOh = nc.dram_tensor("iotas", [128, 16], F32, kind="ExternalInput").ap()
    IDh = nc.dram_tensor("ident", [128, 128], F32, kind="ExternalInput").ap()
    CSh = nc.dram_tensor("csts", [128, 2], F32, kind="ExternalInput").ap()
    Yh = nc.dram_tensor("y", [SEQ, D], F32, kind="ExternalOutput").ap()

    p = Prog()

    with ExitStack() as ctx:
        E = ctx.enter_context
        sb = lambda name, cols, dt: E(nc.sbuf_tensor(name, [128, cols], dt))

        XN = sb("XN", NDC * 1024, BF16)
        Y = sb("Y", NT * 1024, F32)
        W13 = sb("W13", 2 * 8192, BF16)
        W2 = sb("W2", 2 * 4096, BF16)
        WA = sb("WA", 2 * 8192, BF16)
        WO = sb("WO", 8192, BF16)
        RW = sb("RW", 64, BF16)
        Qb = sb("Qb", NDC * 512, BF16)
        Kb = sb("Kb", NDC * 512, BF16)
        Kc = sb("Kc", NDC * 128, BF16)
        Vb = sb("Vb", 4 * 1024, BF16)
        Vc = sb("Vc", 1024, BF16)
        AFb = sb("AFb", NDC * 128, BF16)
        PEx = sb("PEx", 2 * 256, F32)
        PTs = sb("PTs", 8 * 128, BF16)
        PMs = sb("PMs", 4 * 256, BF16)
        RS = sb("RS", 8, F32)
        RD = sb("RD", 8, F32)
        HB = sb("HB", 2 * 2048, BF16)
        SIL = sb("SIL", 2 * 512, F32)
        XT = sb("XT", 2 * 1024, F32)
        ST = sb("ST", 48, F32)
        RTR = sb("RTR", 64, F32)
        CMB = sb("CMB", NT * 8, F32)
        IDN = sb("IDN", 128, BF16)
        MKs = sb("MKs", 512, F32)
        IOs = sb("IOs", 16, F32)
        BBs = sb("BBs", 8, F32)
        CST = sb("CST", 2, F32)

        SQ = SIL
        XTOK = (PMs, PTs)

        pA = E(nc.psum_tensor("pA", [128, 512], F32))
        pB = E(nc.psum_tensor("pB", [128, 512], F32))
        pC = E(nc.psum_tensor("pC", [128, 1024], BF16))
        pD = E(nc.psum_tensor("pD", [128, 512], F32))
        pEF = E(nc.psum_tensor("pEF", [128, 1024], F32))
        pGH = E(nc.psum_tensor("pGH", [128, 1024], F32))

        sems = {n: E(nc.semaphore(n)) for n in
                ("wconst", "wa0", "wa1", "wf0", "wf1", "xd0", "xd1",
                 "stq", "pe", "act", "dve", "pool")}

        pc_rel = [0]
        sc_rel = {0: 0, 1: 0}
        h1_rel = [0]
        h3_rel = [0]
        ef_rel = {0: 0, 1: 0}
        ef_par = [0]
        pex_rel = {0: 0, 1: 0}
        lg_rel = [0]
        af_rel = [0]

        p.emit("pool", lambda g: g.dma_start(out=IDN[:], in_=IDh), inc=("wconst", 16))
        p.emit("pool", lambda g: g.dma_start(out=MKs[:], in_=MKh), inc=("wconst", 16))
        p.emit("pool", lambda g: g.dma_start(out=IOs[:], in_=IOh), inc=("wconst", 16))
        p.emit("pool", lambda g: g.dma_start(out=BBs[:], in_=BBh), inc=("wconst", 16))
        p.emit("pool", lambda g: g.dma_start(out=RW[:], in_=RWh), inc=("wconst", 16))
        p.emit("pool", lambda g: g.dma_start(out=WO[:], in_=WOh), inc=("wconst", 16))
        p.emit("pool", lambda g: g.dma_start(out=CST[:], in_=CSh), inc=("wconst", 16))
        consts_w = p.now("wconst")
        p.emit("dve", lambda v: v.memset(Kc[:], 0), inc=("dve", 1))
        p.emit("dve", lambda v: v.memset(Vc[:], 0), inc=("dve", 1))

        wa_seq = [0]
        wa_last_use = [0, 0]
        wa_ready = [0, 0]

        def load_wa(widx):
            b = wa_seq[0] % 2
            wv = p.emit("pool",
                        (lambda g, b=b, widx=widx: g.dma_start(
                            out=WA[:, b * 8192:(b + 1) * 8192], in_=WAh[widx])),
                        waits=[("pe", wa_last_use[b])], inc=("wa%d" % b, 16))
            wa_ready[b] = wv
            wa_seq[0] += 1
            return b

        f_seq = [0]
        f_last_use = [0, 0]
        f_ready = [0, 0]

        def load_f(e, hg):
            b = f_seq[0] % 2
            p.emit("pool",
                   (lambda g, b=b, e=e, hg=hg: g.dma_start(
                       out=W13[:, b * 8192:(b + 1) * 8192], in_=W13h[e, hg])),
                   waits=[("pe", f_last_use[b])], inc=("wf%d" % b, 16))
            wv = p.emit("pool",
                        (lambda g, b=b, e=e, hg=hg: g.dma_start(
                            out=W2[:, b * 4096:(b + 1) * 4096], in_=W2h[e, hg])),
                        inc=("wf%d" % b, 16))
            f_ready[b] = wv
            f_seq[0] += 1

        store_done = np.zeros((NQ, NT), dtype=np.int64)
        kcar_done = [0]
        vcar_done = [0]
        y_done = np.zeros((NQ, NT), dtype=np.int64)

        for q in range(NQ):
            xa_done = [0, 0]
            xd_done = [0, 0]
            xload = [0, 0]
            for j in range(NT):
                par = j % 2
                xload[par] = p.emit(
                    "sp",
                    (lambda e, q=q, j=j, par=par: e.dma_start(
                        out=XT[:, par * 1024:(par + 1) * 1024],
                        in_=Xh[q * 1024 + j * 128: q * 1024 + (j + 1) * 128, :])),
                    waits=[("act", xa_done[par]), ("dve", xd_done[par])],
                    inc=("xd%d" % par, 16))
                sq_wl = [("xd%d" % par, xload[par])]
                if j == 0:
                    sq_wl.append(("dve", h3_rel[0]))
                xa_done[par] = p.emit(
                    "act",
                    (lambda s, j=j, par=par: s.activation(
                        SQ[:], XT[:, par * 1024:(par + 1) * 1024],
                        AF.Square, accum_out=ST[:, 8 + j:9 + j])),
                    waits=sq_wl, inc=("act", 1))
                xd_done[par] = p.emit(
                    "dve",
                    (lambda v, j=j, par=par: v.reduce_sum(
                        ST[:, j:j + 1], XT[:, par * 1024:(par + 1) * 1024], AX.X)),
                    waits=[("xd%d" % par, xload[par])], inc=("dve", 1))
            p.emit("dve", lambda v: v.tensor_scalar(
                ST[:, 16:24], ST[:, 0:8], 1.0 / D, None, ALU.mult),
                waits=[("act", xa_done[0]), ("act", xa_done[1])], inc=("dve", 1))
            p.emit("dve", lambda v: v.tensor_tensor(
                ST[:, 24:32], ST[:, 16:24], ST[:, 16:24], ALU.mult), inc=("dve", 1))
            p.emit("dve", lambda v: v.tensor_scalar(
                ST[:, 32:40], ST[:, 8:16], 1.0 / D, None, ALU.mult), inc=("dve", 1))
            stats_dve = p.emit("dve", lambda v: v.tensor_tensor(
                ST[:, 32:40], ST[:, 32:40], ST[:, 24:32], ALU.subtract), inc=("dve", 1))
            p.emit("act", lambda s: s.activation(
                ST[:, 40:48], ST[:, 32:40], AF.Ln, bias=CST[:, 0:1], scale=1.0),
                waits=[("dve", stats_dve), ("wconst", consts_w)], inc=("act", 1))
            rstd_act = p.emit("act", lambda s: s.activation(
                ST[:, 40:48], ST[:, 40:48], AF.Exp, bias=0.0, scale=-0.5),
                inc=("act", 1))

            xt_done = [xd_done[0], xd_done[1]]
            xt_done_p = [0, 0]
            tok_done = [0, 0]
            for j in range(NT):
                par = j % 2
                xload[par] = p.emit(
                    "sp",
                    (lambda e, q=q, j=j, par=par: e.dma_start(
                        out=XT[:, par * 1024:(par + 1) * 1024],
                        in_=Xh[q * 1024 + j * 128: q * 1024 + (j + 1) * 128, :])),
                    waits=[("dve", xt_done[par]), ("pool", xt_done_p[par]),
                           ("act", xa_done[par])],
                    inc=("xd%d" % par, 16))
                apply_v = p.emit(
                    "dve",
                    (lambda v, j=j, par=par: v.tensor_scalar(
                        XTOK[par][:],
                        XT[:, par * 1024:(par + 1) * 1024],
                        ST[:, 16 + j:17 + j], ST[:, 40 + j:41 + j],
                        ALU.subtract, ALU.mult)),
                    waits=[("xd%d" % par, xload[par]), ("act", rstd_act),
                           ("pe", tok_done[par])],
                    inc=("dve", 1))
                xt_done[par] = apply_v
                xt_done_p[par] = p.emit(
                    "pool",
                    (lambda g, j=j, par=par: g.tensor_copy(
                        Y[:, j * 1024:(j + 1) * 1024],
                        XT[:, par * 1024:(par + 1) * 1024])),
                    waits=[("xd%d" % par, xload[par]),
                           ("stq", 128 * q)],
                    inc=("pool", 1))
                for dc in range(NDC):
                    wl = [("dve", apply_v)]
                    if dc == 0:
                        wl += [("act", pc_rel[0]), ("wconst", consts_w)]
                    mi = p.emit(
                        "pe",
                        (lambda t, dc=dc, par=par: t.transpose(
                            pC[:, dc * 128:(dc + 1) * 128],
                            XTOK[par][:, dc * 128:(dc + 1) * 128],
                            IDN[:])),
                        waits=wl, inc=("pe", 1) if dc == NDC - 1 else None)
                tok_done[par] = mi
                pc_rel[0] = p.emit(
                    "act",
                    (lambda s, j=j: s.activation(
                        XN.rearrange("p (dc t) -> p dc t", dc=NDC)[:, :, j * 128:(j + 1) * 128],
                        pC.rearrange("p (dc t) -> p dc t", dc=NDC),
                        AF.Copy)),
                    waits=[("pe", tok_done[par])], inc=("act", 1))
            ln_done_act = pc_rel[0]

            for j in range(NT if "r" in phases else 0):
                for dc in range(NDC):
                    wl = [("act", ln_done_act), ("act", lg_rel[0])] if dc == 0 else ()
                    mms = p.emit(
                        "pe",
                        (lambda t, j=j, dc=dc: t.matmul(
                            pD[:, 0:8],
                            XN[:, dc * 1024 + j * 128: dc * 1024 + (j + 1) * 128],
                            RW[:, dc * 8:(dc + 1) * 8],
                            start=(dc == 0), stop=(dc == NDC - 1))),
                        waits=wl, inc=("pe", 1) if dc == NDC - 1 else None)
                lg_rel[0] = p.emit(
                    "act",
                    lambda s: s.activation(RTR[:, 0:8], pD[:, 0:8], AF.Copy),
                    waits=[("pe", mms)], inc=("act", 1))
                p.emit("dve", lambda v: v.tensor_tensor(
                    RTR[:, 0:8], RTR[:, 0:8], BBs[:], ALU.add),
                    waits=[("act", lg_rel[0])], inc=("dve", 1))
                p.emit("dve", lambda v: v.reduce_max(RTR[:, 8:9], RTR[:, 0:8], AX.X), inc=("dve", 1))
                p.emit("dve", lambda v: v.tensor_scalar(
                    RTR[:, 16:24], RTR[:, 0:8], RTR[:, 8:9], None, ALU.is_equal), inc=("dve", 1))
                p.emit("dve", lambda v: v.tensor_tensor(
                    RTR[:, 16:24], RTR[:, 16:24], IOs[:, 8:16], ALU.mult), inc=("dve", 1))
                p.emit("dve", lambda v: v.tensor_scalar(
                    RTR[:, 16:24], RTR[:, 16:24], -1.0, 8.0, ALU.mult, ALU.add), inc=("dve", 1))
                p.emit("dve", lambda v: v.tensor_reduce(
                    RTR[:, 9:10], RTR[:, 16:24], AX.X, ALU.min), inc=("dve", 1))
                p.emit("dve", lambda v: v.tensor_scalar(
                    RTR[:, 24:32], IOs[:, 0:8], RTR[:, 9:10], None, ALU.is_equal), inc=("dve", 1))
                p.emit("dve", lambda v: v.scalar_tensor_tensor(
                    RTR[:, 32:40], RTR[:, 24:32], NEGBIG, RTR[:, 0:8], ALU.mult, ALU.add), inc=("dve", 1))
                p.emit("dve", lambda v: v.reduce_max(RTR[:, 10:11], RTR[:, 32:40], AX.X), inc=("dve", 1))
                p.emit("dve", lambda v: v.tensor_scalar(
                    RTR[:, 40:48], RTR[:, 32:40], RTR[:, 10:11], None, ALU.is_equal), inc=("dve", 1))
                p.emit("dve", lambda v: v.tensor_tensor(
                    RTR[:, 40:48], RTR[:, 40:48], IOs[:, 8:16], ALU.mult), inc=("dve", 1))
                p.emit("dve", lambda v: v.tensor_scalar(
                    RTR[:, 40:48], RTR[:, 40:48], -1.0, 8.0, ALU.mult, ALU.add), inc=("dve", 1))
                p.emit("dve", lambda v: v.tensor_reduce(
                    RTR[:, 11:12], RTR[:, 40:48], AX.X, ALU.min), inc=("dve", 1))
                p.emit("dve", lambda v: v.tensor_scalar(
                    RTR[:, 40:48], IOs[:, 0:8], RTR[:, 11:12], None, ALU.is_equal), inc=("dve", 1))
                dv_ = p.emit("dve", lambda v: v.tensor_tensor(
                    RTR[:, 12:13], RTR[:, 10:11], RTR[:, 8:9], ALU.subtract), inc=("dve", 1))
                e_act = p.emit("act", lambda s: s.activation(
                    RTR[:, 13:14], RTR[:, 12:13], AF.Exp),
                    waits=[("dve", dv_)], inc=("act", 1))
                p.emit("dve", lambda v: v.tensor_scalar(
                    RTR[:, 14:15], RTR[:, 13:14], 1.0, None, ALU.add),
                    waits=[("act", e_act)], inc=("dve", 1))
                rec_v = p.emit("dve", lambda v: v.reciprocal(RTR[:, 15:16], RTR[:, 14:15]),
                               inc=("dve", 1))
                p.emit("dve", lambda v: v.tensor_scalar(
                    RTR[:, 48:56], RTR[:, 24:32], RTR[:, 15:16], None, ALU.mult),
                    waits=[("dve", rec_v)], inc=("dve", 1))
                p.emit("dve", lambda v: v.tensor_tensor(
                    RTR[:, 13:14], RTR[:, 13:14], RTR[:, 15:16], ALU.mult), inc=("dve", 1))
                p.emit("dve", (lambda v, j=j: v.scalar_tensor_tensor(
                    CMB[:, j * 8:(j + 1) * 8], RTR[:, 40:48], RTR[:, 13:14],
                    RTR[:, 48:56], ALU.mult, ALU.add)), inc=("dve", 1))

            for g in range(NG if "a" in phases else 0):
                bq = load_wa(0)
                bk = load_wa(1)
                for dqc in range(NDC):
                    par = ef_par[0]
                    bank = (pEF, pGH)[par]
                    for dc in range(NDC):
                        wl = [("wa%d" % bq, wa_ready[bq]), ("act", ln_done_act),
                              ("dve", ef_rel[par])] if dc == 0 else ()
                        mi = p.emit(
                            "pe",
                            (lambda t, bank=bank, dqc=dqc, dc=dc, g=g, bq=bq: t.matmul(
                                bank[:, 0:512],
                                WA[:, bq * 8192 + dc * 1024 + dqc * 128:
                                   bq * 8192 + dc * 1024 + (dqc + 1) * 128],
                                XN[:, dc * 1024 + g * 512: dc * 1024 + (g + 1) * 512],
                                start=(dc == 0), stop=(dc == NDC - 1))),
                            waits=wl, inc=("pe", 1) if dc == NDC - 1 else None)
                    ef_rel[par] = p.emit(
                        "dve",
                        (lambda v, bank=bank, dqc=dqc: v.tensor_copy(
                            Qb[:, dqc * 512:(dqc + 1) * 512], bank[:, 0:512])),
                        waits=[("pe", mi)], inc=("dve", 1))
                    ef_par[0] ^= 1
                wa_last_use[bq] = p.now("pe")
                q_done = p.now("dve")
                for dkc in range(NDC):
                    par = ef_par[0]
                    bank = (pEF, pGH)[par]
                    for dc in range(NDC):
                        wl = [("wa%d" % bk, wa_ready[bk]), ("dve", ef_rel[par]),
                              ("pool", kcar_done[0])] if dc == 0 else ()
                        mi = p.emit(
                            "pe",
                            (lambda t, bank=bank, dkc=dkc, dc=dc, g=g, bk=bk: t.matmul(
                                bank[:, 0:512],
                                WA[:, bk * 8192 + dc * 1024 + dkc * 128:
                                   bk * 8192 + dc * 1024 + (dkc + 1) * 128],
                                XN[:, dc * 1024 + g * 512: dc * 1024 + (g + 1) * 512],
                                start=(dc == 0), stop=(dc == NDC - 1))),
                            waits=wl, inc=("pe", 1) if dc == NDC - 1 else None)
                    ef_rel[par] = p.emit(
                        "dve",
                        (lambda v, bank=bank, dkc=dkc: v.tensor_copy(
                            Kb[:, dkc * 512:(dkc + 1) * 512], bank[:, 0:512])),
                        waits=[("pe", mi)], inc=("dve", 1))
                    ef_par[0] ^= 1
                bv = load_wa(2)
                wa_last_use[bk] = p.now("pe")
                k_done = p.now("dve")
                for tl in range(4):
                    par = ef_par[0]
                    bank = (pEF, pGH)[par]
                    for half in range(2):
                        for dc in range(NDC):
                            wl = [("wa%d" % bv, wa_ready[bv]), ("dve", ef_rel[par]),
                                  ("pool", vcar_done[0])] if (dc == 0 and half == 0) else ()
                            mi = p.emit(
                                "pe",
                                (lambda t, bank=bank, tl=tl, dc=dc, g=g, bv=bv, half=half: t.matmul(
                                    bank[:, half * 512:(half + 1) * 512],
                                    XN[:, dc * 1024 + g * 512 + tl * 128:
                                       dc * 1024 + g * 512 + (tl + 1) * 128],
                                    WA[:, bv * 8192 + dc * 1024 + half * 512:
                                       bv * 8192 + dc * 1024 + (half + 1) * 512],
                                    start=(dc == 0), stop=(dc == NDC - 1))),
                                waits=wl,
                                inc=("pe", 1) if (dc == NDC - 1 and half == 1) else None)
                    ef_rel[par] = p.emit(
                        "dve",
                        (lambda v, bank=bank, tl=tl: v.tensor_copy(
                            Vb[:, tl * 1024:(tl + 1) * 1024], bank[:, 0:1024])),
                        waits=[("pe", mi)], inc=("dve", 1))
                    ef_par[0] ^= 1
                wa_last_use[bv] = p.now("pe")
                v_done = p.now("dve")

                if q == 0 and g == 0:
                    load_f(0, 0)
                    load_f(0, 1)

                for cl in range(4 if alvl >= 2 else 0):
                    c = g * 4 + cl
                    gq0 = (q == 0 and c == 0)
                    mask_ap = MKs[:, 256:512] if gq0 else MKs[:, 0:256]
                    pt_evac = 0
                    for hgrp in range(4):
                        pn_ready = {}
                        for hh in range(4):
                            h = hgrp * 4 + hh
                            bidx = hh % 2
                            sbank = (pA, pB)[bidx]
                            rowb = (h % 2) * 64
                            dkc = h // 2
                            wl = [("dve", k_done), ("act", sc_rel[bidx])]
                            if cl == 0:
                                p.emit(
                                    "pe",
                                    (lambda t, sbank=sbank, rowb=rowb, dkc=dkc, cl=cl: t.matmul(
                                        sbank[:, 0:128],
                                        Qb[rowb:rowb + 64, dkc * 512 + cl * 128: dkc * 512 + (cl + 1) * 128],
                                        Kc[rowb:rowb + 64, dkc * 128:(dkc + 1) * 128],
                                        start=True, stop=True)),
                                    waits=wl)
                                mi = p.emit(
                                    "pe",
                                    (lambda t, sbank=sbank, rowb=rowb, dkc=dkc, cl=cl: t.matmul(
                                        sbank[:, 128:256],
                                        Qb[rowb:rowb + 64, dkc * 512 + cl * 128: dkc * 512 + (cl + 1) * 128],
                                        Kb[rowb:rowb + 64, dkc * 512: dkc * 512 + 128],
                                        start=True, stop=True)),
                                    inc=("pe", 1))
                            else:
                                mi = p.emit(
                                    "pe",
                                    (lambda t, sbank=sbank, rowb=rowb, dkc=dkc, cl=cl: t.matmul(
                                        sbank[:, 0:256],
                                        Qb[rowb:rowb + 64, dkc * 512 + cl * 128: dkc * 512 + (cl + 1) * 128],
                                        Kb[rowb:rowb + 64, dkc * 512 + (cl - 1) * 128: dkc * 512 + (cl + 1) * 128],
                                        start=True, stop=True)),
                                    waits=wl, inc=("pe", 1))
                            ep = hh % 2
                            sc_rel[bidx] = p.emit(
                                "act",
                                (lambda s, sbank=sbank, ep=ep: s.activation(
                                    PEx[:, ep * 256:(ep + 1) * 256],
                                    sbank[:, 0:256],
                                    AF.Exp, bias=CST[:, 1:2], scale=0.125)),
                                waits=[("pe", mi), ("dve", pex_rel[ep])],
                                inc=("act", 1))
                            pex_rel[ep] = p.emit(
                                "dve",
                                (lambda v, ep=ep, hh=hh, mask_ap=mask_ap: v.scalar_tensor_tensor(
                                    PMs[:, hh * 256:(hh + 1) * 256],
                                    PEx[:, ep * 256:(ep + 1) * 256], 1.0, mask_ap,
                                    ALU.mult, ALU.mult,
                                    accum_out=RS[:, hh:hh + 1])),
                                waits=[("act", sc_rel[bidx])], inc=("dve", 1))
                        rec_v = p.emit("dve", lambda v: v.reciprocal(RD[:, 0:4], RS[:, 0:4]),
                                       inc=("dve", 1))
                        for hh in range(4):
                            pn_ready[hh] = p.emit(
                                "dve",
                                (lambda v, hh=hh: v.tensor_scalar(
                                    PMs[:, hh * 256:(hh + 1) * 256],
                                    PMs[:, hh * 256:(hh + 1) * 256],
                                    RD[:, hh:hh + 1], None, ALU.mult)),
                                waits=[("dve", rec_v)], inc=("dve", 1))
                        for hh in range(4 if alvl >= 3 else 0):
                            for kk in range(2):
                                wl = [("dve", pn_ready[hh])]
                                if hh == 0 and kk == 0:
                                    wl.append(("act", pc_rel[0]))
                                mi = p.emit(
                                    "pe",
                                    (lambda t, hh=hh, kk=kk: t.transpose(
                                        pC[:, (hh * 2 + kk) * 128:(hh * 2 + kk + 1) * 128],
                                        PMs[:, hh * 256 + kk * 128: hh * 256 + (kk + 1) * 128],
                                        IDN[:])),
                                    waits=wl,
                                    inc=("pe", 1) if (hh == 3 and kk == 1) else None)
                        if alvl >= 3:
                            pc_rel[0] = pt_evac = p.emit(
                                "act",
                                lambda s: s.activation(PTs[:], pC[:], AF.Copy),
                                waits=[("pe", mi)], inc=("act", 1))
                        o_mi = {}
                        for hh in range(4 if alvl >= 4 else 0):
                            h = hgrp * 4 + hh
                            if cl == 0:
                                lhs_prev = Vc[:, h * 64:(h + 1) * 64]
                            else:
                                lhs_prev = Vb[:, (cl - 1) * 1024 + h * 64:
                                              (cl - 1) * 1024 + (h + 1) * 64]
                            p.emit(
                                "pe",
                                (lambda t, lhs_prev=lhs_prev, hh=hh: t.matmul(
                                    pD[0:64, hh * 128:(hh + 1) * 128],
                                    lhs_prev, PTs[:, (hh * 2) * 128:(hh * 2 + 1) * 128],
                                    start=True, stop=False)),
                                waits=([("act", pt_evac), ("dve", v_done),
                                        ("act", af_rel[0])] if hh == 0 else
                                       [("act", pt_evac), ("dve", v_done)]))
                            o_mi[hh] = p.emit(
                                "pe",
                                (lambda t, cl=cl, h=h, hh=hh: t.matmul(
                                    pD[0:64, hh * 128:(hh + 1) * 128],
                                    Vb[:, cl * 1024 + h * 64: cl * 1024 + (h + 1) * 64],
                                    PTs[:, (hh * 2 + 1) * 128:(hh * 2 + 2) * 128],
                                    start=False, stop=True)),
                                inc=("pe", 1))
                        for hh in range(4 if alvl >= 4 else 0):
                            h = hgrp * 4 + hh
                            rowb = (h % 2) * 64
                            dc = h // 2
                            af_last = p.emit(
                                "act",
                                (lambda s, rowb=rowb, dc=dc, hh=hh: s.activation(
                                    AFb[rowb:rowb + 64, dc * 128:(dc + 1) * 128],
                                    pD[0:64, hh * 128:(hh + 1) * 128], AF.Copy)),
                                waits=[("pe", o_mi[3])], inc=("act", 1))
                            af_rel[0] = af_last
                    if alvl < 5:
                        continue
                    par = ef_par[0]
                    wo_bank = (pEF, pGH)[par]
                    for dc in range(NDC):
                        for half in range(2):
                            wl = [("act", af_last), ("wconst", consts_w),
                                  ("dve", ef_rel[par])] if (dc == 0 and half == 0) else ()
                            mi = p.emit(
                                "pe",
                                (lambda t, wo_bank=wo_bank, dc=dc, half=half: t.matmul(
                                    wo_bank[:, half * 512:(half + 1) * 512],
                                    AFb[:, dc * 128:(dc + 1) * 128],
                                    WO[:, dc * 1024 + half * 512: dc * 1024 + (half + 1) * 512],
                                    start=(dc == 0), stop=(dc == NDC - 1))),
                                waits=wl,
                                inc=("pe", 1) if (dc == NDC - 1 and half == 1) else None)
                    ef_rel[par] = p.emit(
                        "dve",
                        (lambda v, wo_bank=wo_bank, c=c: v.scalar_tensor_tensor(
                            Y[:, c * 1024:(c + 1) * 1024], wo_bank[:, 0:1024], 1.0,
                            Y[:, c * 1024:(c + 1) * 1024], ALU.mult, ALU.add)),
                        waits=[("pe", mi), ("pool", max(xt_done_p))],
                        inc=("dve", 1))
                    ef_par[0] ^= 1
                kc_v = None
                pe_now = p.now("pe")
                for dkc in range(NDC):
                    kc_v = p.emit(
                        "pool",
                        (lambda g_, dkc=dkc: g_.tensor_copy(
                            Kc[:, dkc * 128:(dkc + 1) * 128],
                            Kb[:, dkc * 512 + 384: dkc * 512 + 512])),
                        waits=[("pe", pe_now)], inc=("pool", 1))
                vc_v = p.emit(
                    "pool",
                    lambda g_: g_.tensor_copy(Vc[:], Vb[:, 3 * 1024:4 * 1024]),
                    inc=("pool", 1))
                kcar_done[0] = kc_v
                vcar_done[0] = vc_v

            act_barrier = p.now("act")
            dve_barrier = p.now("dve")
            if "f" not in phases:
                for j in range(NT):
                    y_done[q][j] = p.now("pool")
            for e in range(NE if "f" in phases else 0):
                for hg in range(NHG):
                    idx = e * NHG + hg
                    fb = idx % 2
                    fw = f_ready[fb]
                    fwsem = "wf%d" % fb
                    for tt in range(2):
                        hbb = tt
                        hb_done = 0
                        for hc in range(NHC):
                            for dc in range(NDC):
                                wl = [(fwsem, fw),
                                      ("act", max(h1_rel[0], sc_rel[0], act_barrier if (idx == 0 and tt == 0 and hc == 0) else 0))] if dc == 0 else ()
                                mi1 = p.emit(
                                    "pe",
                                    (lambda t, fb=fb, dc=dc, hc=hc, tt=tt: t.matmul(
                                        pA[:, 0:512],
                                        W13[:, fb * 8192 + dc * 1024 + hc * 128:
                                            fb * 8192 + dc * 1024 + (hc + 1) * 128],
                                        XN[:, dc * 1024 + tt * 512: dc * 1024 + (tt + 1) * 512],
                                        start=(dc == 0), stop=(dc == NDC - 1))),
                                    waits=wl, inc=("pe", 1) if dc == NDC - 1 else None)
                            for dc in range(NDC):
                                wl = [("dve", h3_rel[0]),
                                      ("act", sc_rel[1] if (idx == 0 and tt == 0 and hc == 0) else 0)] if dc == 0 else ()
                                mi3 = p.emit(
                                    "pe",
                                    (lambda t, fb=fb, dc=dc, hc=hc, tt=tt: t.matmul(
                                        pB[:, 0:512],
                                        W13[:, fb * 8192 + dc * 1024 + 512 + hc * 128:
                                            fb * 8192 + dc * 1024 + 512 + (hc + 1) * 128],
                                        XN[:, dc * 1024 + tt * 512: dc * 1024 + (tt + 1) * 512],
                                        start=(dc == 0), stop=(dc == NDC - 1))),
                                    waits=wl, inc=("pe", 1) if dc == NDC - 1 else None)
                            sp_ = hc % 2
                            h1_rel[0] = p.emit(
                                "act",
                                (lambda s, sp_=sp_: s.activation(
                                    SIL[:, sp_ * 512:(sp_ + 1) * 512], pA[:, 0:512], AF.Silu)),
                                waits=[("pe", mi1), ("dve", h3_rel[0])], inc=("act", 1))
                            h3_rel[0] = hb_done = p.emit(
                                "dve",
                                (lambda v, sp_=sp_, hbb=hbb, hc=hc: v.tensor_tensor(
                                    HB[:, hbb * 2048 + hc * 512: hbb * 2048 + (hc + 1) * 512],
                                    SIL[:, sp_ * 512:(sp_ + 1) * 512], pB[:, 0:512], ALU.mult)),
                                waits=[("act", h1_rel[0]), ("pe", mi3)], inc=("dve", 1))
                        for jj in range(4):
                            j = tt * 4 + jj
                            par = ef_par[0]
                            bank = (pEF, pGH)[par]
                            for hc in range(NHC):
                                for half in range(2):
                                    wl = [("dve", max(hb_done, ef_rel[par]))] if (hc == 0 and half == 0) else ()
                                    mi = p.emit(
                                        "pe",
                                        (lambda t, bank=bank, fb=fb, hbb=hbb, hc=hc, jj=jj, half=half: t.matmul(
                                            bank[:, half * 512:(half + 1) * 512],
                                            HB[:, hbb * 2048 + hc * 512 + jj * 128:
                                               hbb * 2048 + hc * 512 + (jj + 1) * 128],
                                            W2[:, fb * 4096 + hc * 1024 + half * 512:
                                               fb * 4096 + hc * 1024 + (half + 1) * 512],
                                            start=(hc == 0), stop=(hc == NHC - 1))),
                                        waits=wl,
                                        inc=("pe", 1) if (hc == NHC - 1 and half == 1) else None)
                            if e < 2:
                                ef_rel[par] = p.emit(
                                    "dve",
                                    (lambda v, bank=bank, j=j: v.scalar_tensor_tensor(
                                        Y[:, j * 1024:(j + 1) * 1024], bank[:, 0:1024], 1.0,
                                        Y[:, j * 1024:(j + 1) * 1024], ALU.mult, ALU.add)),
                                    waits=[("pe", mi)], inc=("dve", 1))
                            else:
                                ef_rel[par] = p.emit(
                                    "dve",
                                    (lambda v, bank=bank, j=j, e=e: v.scalar_tensor_tensor(
                                        Y[:, j * 1024:(j + 1) * 1024], bank[:, 0:1024],
                                        CMB[:, j * 8 + (e - 2): j * 8 + (e - 1)],
                                        Y[:, j * 1024:(j + 1) * 1024], ALU.mult, ALU.add)),
                                    waits=[("pe", mi)], inc=("dve", 1))
                            if e == NE - 1 and hg == NHG - 1:
                                y_done[q][j] = ef_rel[par]
                            ef_par[0] ^= 1
                    f_last_use[fb] = p.now("pe")
                    nidx = idx + 2
                    if nidx < NE * NHG:
                        ne_, nhg_ = divmod(nidx, NHG)
                        load_f(ne_, nhg_)
                    elif q < NQ - 1:
                        ne_, nhg_ = divmod(nidx - NE * NHG, NHG)
                        load_f(ne_, nhg_)

            sc_rel[0] = max(sc_rel[0], h1_rel[0])
            store_sem = "dve" if "f" in phases else "pool"
            for j in range(NT):
                store_done[q][j] = p.emit(
                    "sp",
                    (lambda e_, q=q, j=j: e_.dma_start(
                        out=Yh[q * 1024 + j * 128: q * 1024 + (j + 1) * 128, :],
                        in_=Y[:, j * 1024:(j + 1) * 1024])),
                    waits=[(store_sem, y_done[q][j])], inc=("stq", 16))

        p.wait_only("sp", [("stq", p.now("stq"))])

        def replay(lst):
            def f(eng):
                waited = {}
                for fn, waits, inc in lst:
                    for (s, v) in waits:
                        if waited.get(s, 0) < v:
                            eng.wait_ge(sems[s], int(v))
                            waited[s] = v
                    if fn is None:
                        continue
                    ins = fn(eng)
                    if inc is not None and ins is not None:
                        ins.then_inc(sems[inc[0]], inc[1])
            return f

        block = E(nc.Block())
        block.tensor(replay(p.ops["pe"]))
        block.scalar(replay(p.ops["act"]))
        block.vector(replay(p.ops["dve"]))
        block.gpsimd(replay(p.ops["pool"]))
        block.sync(replay(p.ops["sp"]))

    return nc



def _prep_weights(ins):
    f32 = np.float32

    def wt_blocks(WT):
        return np.ascontiguousarray(
            WT.reshape(8, 128, 1024).transpose(1, 0, 2).reshape(128, 8192))

    WAarr = np.ascontiguousarray(np.stack([
        wt_blocks(np.ascontiguousarray(ins["L0_attn_Wq"].T)),
        wt_blocks(np.ascontiguousarray(ins["L0_attn_Wk"].T)),
        wt_blocks(np.ascontiguousarray(ins["L0_attn_Wv"].T)),
    ])).astype(f32)
    WOarr = wt_blocks(np.ascontiguousarray(ins["L0_attn_Wo"].T)).astype(f32)

    w1 = np.concatenate([ins["L0_ffn_sw1"], ins["L0_ffn_w1"]], axis=0)
    w3 = np.concatenate([ins["L0_ffn_sw3"], ins["L0_ffn_w3"]], axis=0)
    w2 = np.concatenate([ins["L0_ffn_sw2"], ins["L0_ffn_w2"]], axis=0)

    w1T = np.transpose(w1, (0, 2, 1)).reshape(NE, 8, 128, 4, 512)
    w3T = np.transpose(w3, (0, 2, 1)).reshape(NE, 8, 128, 4, 512)
    a1 = np.transpose(w1T, (0, 3, 2, 1, 4))
    a3 = np.transpose(w3T, (0, 3, 2, 1, 4))
    W13arr = np.ascontiguousarray(
        np.concatenate([a1, a3], axis=4).reshape(NE, NHG, 128, 8192)).astype(f32)

    w2T = np.transpose(w2, (0, 2, 1)).reshape(NE, 4, 4, 128, 1024)
    W2arr = np.ascontiguousarray(
        np.transpose(w2T, (0, 1, 3, 2, 4)).reshape(NE, NHG, 128, 4096)).astype(f32)

    rwT = np.ascontiguousarray(ins["L0_ffn_router_w"].T).reshape(8, 128, 8)
    RWarr = np.ascontiguousarray(rwT.transpose(1, 0, 2).reshape(128, 64)).astype(f32)

    BBarr = np.ascontiguousarray(
        np.tile(np.asarray(ins["L0_ffn_expert_bias"]).reshape(1, 8), (128, 1))).astype(f32)

    i = np.arange(128)[:, None]
    j = np.arange(256)[None, :]
    band = ((j >= i + 1) & (j <= i + 128)).astype(f32)
    first = (band * (j >= 128)).astype(f32)
    MKarr = np.ascontiguousarray(np.concatenate([band, first], axis=1)).astype(f32)

    io = np.tile(np.arange(8, dtype=f32).reshape(1, 8), (128, 1))
    IOarr = np.ascontiguousarray(np.concatenate([io, 8.0 - io], axis=1)).astype(f32)

    IDarr = np.eye(128, dtype=f32)
    CSarr = np.ascontiguousarray(
        np.tile(np.asarray([[1e-5, EXP_SHIFT]], dtype=f32), (128, 1)))

    return dict(wa=WAarr, wo=WOarr, wf13=W13arr, wf2=W2arr, rwt=RWarr,
                biasb=BBarr, masks=MKarr, iotas=IOarr, ident=IDarr, csts=CSarr)


_NC_CACHE = {}


def kernel(**inputs):
    x = np.asarray(inputs["L0_attn_x"], dtype=np.float32)
    shared = _prep_weights({k: np.asarray(v) for k, v in inputs.items()})

    if "main" not in _NC_CACHE:
        _NC_CACHE["main"] = build_nc()
    nc = _NC_CACHE["main"]

    in_maps = []
    for c in range(8):
        m = dict(shared)
        m["x"] = np.ascontiguousarray(x[c])
        in_maps.append(m)

    res = run_bass_kernel_spmd(nc, in_maps, core_ids=list(range(8)))
    out = np.stack([res.results[c]["y"] for c in range(8)], axis=0)
    return out.astype(np.float32)


if __name__ == "__main__":
    build_nc()
    print("built ok")


# version 02
# speedup vs baseline: 1.0124x; 1.0124x over previous
import numpy as np
from contextlib import ExitStack

import concourse.bass as bass
import concourse.mybir as mybir
from concourse.bass_utils import run_bass_kernel_spmd

F32 = mybir.dt.float32
BF16 = mybir.dt.bfloat16
AF = mybir.ActivationFunctionType
ALU = mybir.AluOpType
AX = mybir.AxisListType

SEQ, D = 4096, 1024
NQ, TQ, NT = 4, 1024, 8
NG = 2
NE, NHG, NHC = 10, 4, 4
NDC = 8
EXP_SHIFT = -2.0
NEGBIG = -1e30

ENGINES = ("pe", "act", "dve", "pool", "sp")


class Prog:

    def __init__(self):
        self.ops = {e: [] for e in ENGINES}
        self.cnt = {}

    def emit(self, eng, fn, waits=(), inc=None):
        waits = list(waits)
        if eng in ("act", "dve"):
            if inc is None:
                inc = (eng, 1)
            prev = self.cnt.get(eng, 0)
            if prev:
                waits.append((eng, prev))
        elif eng == "pool" and inc is not None and inc[0] == "pool":
            prev = self.cnt.get("pool", 0)
            if prev:
                waits.append(("pool", prev))
        val = None
        if inc is not None:
            s, a = inc
            self.cnt[s] = self.cnt.get(s, 0) + a
            val = self.cnt[s]
        self.ops[eng].append((fn, [(s, v) for (s, v) in waits if v > 0], inc))
        return val

    def wait_only(self, eng, waits):
        self.ops[eng].append((None, [(s, v) for (s, v) in waits if v > 0], None))

    def now(self, sem):
        return self.cnt.get(sem, 0)


def build_nc(phases="lraf"):
    alvl = 5
    import re as _re
    _m = _re.search(r"a(\d)", phases)
    if _m:
        alvl = int(_m.group(1))
    nc = bass.Bass("TRN2", target_bir_lowering=False)

    Xh = nc.dram_tensor("x", [SEQ, D], F32, kind="ExternalInput").ap()
    WAh = nc.dram_tensor("wa", [3, 128, 8192], F32, kind="ExternalInput").ap()
    WOh = nc.dram_tensor("wo", [128, 8192], F32, kind="ExternalInput").ap()
    W13h = nc.dram_tensor("wf13", [NE, NHG, 128, 8192], F32, kind="ExternalInput").ap()
    W2h = nc.dram_tensor("wf2", [NE, NHG, 128, 4096], F32, kind="ExternalInput").ap()
    RWh = nc.dram_tensor("rwt", [128, 64], F32, kind="ExternalInput").ap()
    BBh = nc.dram_tensor("biasb", [128, 8], F32, kind="ExternalInput").ap()
    MKh = nc.dram_tensor("masks", [128, 512], F32, kind="ExternalInput").ap()
    IOh = nc.dram_tensor("iotas", [128, 16], F32, kind="ExternalInput").ap()
    IDh = nc.dram_tensor("ident", [128, 128], F32, kind="ExternalInput").ap()
    CSh = nc.dram_tensor("csts", [128, 2], F32, kind="ExternalInput").ap()
    Yh = nc.dram_tensor("y", [SEQ, D], F32, kind="ExternalOutput").ap()

    p = Prog()

    with ExitStack() as ctx:
        E = ctx.enter_context
        sb = lambda name, cols, dt: E(nc.sbuf_tensor(name, [128, cols], dt))

        XN = sb("XN", NDC * 1024, BF16)
        Y = sb("Y", NT * 1024, F32)
        W13 = sb("W13", 2 * 8192, BF16)
        W2 = sb("W2", 2 * 4096, BF16)
        WA = sb("WA", 2 * 8192, BF16)
        WO = sb("WO", 8192, BF16)
        RW = sb("RW", 64, BF16)
        Qb = sb("Qb", NDC * 512, BF16)
        Kb = sb("Kb", NDC * 512, BF16)
        Kc = sb("Kc", NDC * 128, BF16)
        Vb = sb("Vb", 4 * 1024, BF16)
        Vc = sb("Vc", 1024, BF16)
        AFb = sb("AFb", NDC * 128, BF16)
        PEx = sb("PEx", 2 * 256, F32)
        PTs = sb("PTs", 8 * 128, BF16)
        PMs = sb("PMs", 4 * 256, BF16)
        RS = sb("RS", 8, F32)
        RD = sb("RD", 8, F32)
        HB = sb("HB", 2 * 2048, BF16)
        SIL = sb("SIL", 2 * 512, F32)
        XT = sb("XT", 2 * 1024, F32)
        ST = sb("ST", 48, F32)
        RTR = sb("RTR", 64, F32)
        CMB = sb("CMB", NT * 8, F32)
        IDN = sb("IDN", 128, BF16)
        MKs = sb("MKs", 512, F32)
        IOs = sb("IOs", 16, F32)
        BBs = sb("BBs", 8, F32)
        CST = sb("CST", 2, F32)

        SQ = SIL
        XTOK = (PMs, PTs)

        pA = E(nc.psum_tensor("pA", [128, 512], F32))
        pB = E(nc.psum_tensor("pB", [128, 512], F32))
        pC = E(nc.psum_tensor("pC", [128, 1024], BF16))
        pD = E(nc.psum_tensor("pD", [128, 512], F32))
        pEF = E(nc.psum_tensor("pEF", [128, 1024], F32))
        pGH = E(nc.psum_tensor("pGH", [128, 1024], F32))

        sems = {n: E(nc.semaphore(n)) for n in
                ("wconst", "wa0", "wa1", "wf0", "wf1", "xd0", "xd1",
                 "stq", "pe", "act", "dve", "pool")}

        pc_rel = [0]
        sc_rel = {0: 0, 1: 0}
        h1_rel = [0]
        h3_rel = [0]
        ef_rel = {0: 0, 1: 0}
        ef_par = [0]
        pex_rel = {0: 0, 1: 0}
        lg_rel = [0]
        af_rel = [0]

        p.emit("pool", lambda g: g.dma_start(out=IDN[:], in_=IDh), inc=("wconst", 16))
        p.emit("pool", lambda g: g.dma_start(out=MKs[:], in_=MKh), inc=("wconst", 16))
        p.emit("pool", lambda g: g.dma_start(out=IOs[:], in_=IOh), inc=("wconst", 16))
        p.emit("pool", lambda g: g.dma_start(out=BBs[:], in_=BBh), inc=("wconst", 16))
        p.emit("pool", lambda g: g.dma_start(out=RW[:], in_=RWh), inc=("wconst", 16))
        p.emit("pool", lambda g: g.dma_start(out=WO[:], in_=WOh), inc=("wconst", 16))
        p.emit("pool", lambda g: g.dma_start(out=CST[:], in_=CSh), inc=("wconst", 16))
        consts_w = p.now("wconst")
        p.emit("dve", lambda v: v.memset(Kc[:], 0), inc=("dve", 1))
        p.emit("dve", lambda v: v.memset(Vc[:], 0), inc=("dve", 1))

        wa_seq = [0]
        wa_last_use = [0, 0]
        wa_ready = [0, 0]

        def load_wa(widx):
            b = wa_seq[0] % 2
            wv = p.emit("pool",
                        (lambda g, b=b, widx=widx: g.dma_start(
                            out=WA[:, b * 8192:(b + 1) * 8192], in_=WAh[widx])),
                        waits=[("pe", wa_last_use[b])], inc=("wa%d" % b, 16))
            wa_ready[b] = wv
            wa_seq[0] += 1
            return b

        f_seq = [0]
        f_last_use = [0, 0]
        f_ready = [0, 0]

        def load_f(e, hg):
            b = f_seq[0] % 2
            p.emit("pool",
                   (lambda g, b=b, e=e, hg=hg: g.dma_start(
                       out=W13[:, b * 8192:(b + 1) * 8192], in_=W13h[e, hg])),
                   waits=[("pe", f_last_use[b])], inc=("wf%d" % b, 16))
            wv = p.emit("pool",
                        (lambda g, b=b, e=e, hg=hg: g.dma_start(
                            out=W2[:, b * 4096:(b + 1) * 4096], in_=W2h[e, hg])),
                        inc=("wf%d" % b, 16))
            f_ready[b] = wv
            f_seq[0] += 1

        store_done = np.zeros((NQ, NT), dtype=np.int64)
        pending_carry_pe = [None]
        kcar_done = [0]
        vcar_done = [0]
        y_done = np.zeros((NQ, NT), dtype=np.int64)

        for q in range(NQ):
            xa_done = [0, 0]
            xd_done = [0, 0]
            xload = [0, 0]
            for j in range(NT):
                par = j % 2
                xload[par] = p.emit(
                    "sp",
                    (lambda e, q=q, j=j, par=par: e.dma_start(
                        out=XT[:, par * 1024:(par + 1) * 1024],
                        in_=Xh[q * 1024 + j * 128: q * 1024 + (j + 1) * 128, :])),
                    waits=[("act", xa_done[par]), ("dve", xd_done[par])],
                    inc=("xd%d" % par, 16))
                sq_wl = [("xd%d" % par, xload[par])]
                if j == 0:
                    sq_wl.append(("dve", h3_rel[0]))
                xa_done[par] = p.emit(
                    "act",
                    (lambda s, j=j, par=par: s.activation(
                        SQ[:], XT[:, par * 1024:(par + 1) * 1024],
                        AF.Square, accum_out=ST[:, 8 + j:9 + j])),
                    waits=sq_wl, inc=("act", 1))
                xd_done[par] = p.emit(
                    "dve",
                    (lambda v, j=j, par=par: v.reduce_sum(
                        ST[:, j:j + 1], XT[:, par * 1024:(par + 1) * 1024], AX.X)),
                    waits=[("xd%d" % par, xload[par])], inc=("dve", 1))
            p.emit("dve", lambda v: v.tensor_scalar(
                ST[:, 16:24], ST[:, 0:8], 1.0 / D, None, ALU.mult),
                waits=[("act", xa_done[0]), ("act", xa_done[1])], inc=("dve", 1))
            p.emit("dve", lambda v: v.tensor_tensor(
                ST[:, 24:32], ST[:, 16:24], ST[:, 16:24], ALU.mult), inc=("dve", 1))
            p.emit("dve", lambda v: v.tensor_scalar(
                ST[:, 32:40], ST[:, 8:16], 1.0 / D, None, ALU.mult), inc=("dve", 1))
            stats_dve = p.emit("dve", lambda v: v.tensor_tensor(
                ST[:, 32:40], ST[:, 32:40], ST[:, 24:32], ALU.subtract), inc=("dve", 1))
            p.emit("act", lambda s: s.activation(
                ST[:, 40:48], ST[:, 32:40], AF.Ln, bias=CST[:, 0:1], scale=1.0),
                waits=[("dve", stats_dve), ("wconst", consts_w)], inc=("act", 1))
            rstd_act = p.emit("act", lambda s: s.activation(
                ST[:, 40:48], ST[:, 40:48], AF.Exp, bias=0.0, scale=-0.5),
                inc=("act", 1))

            xt_done = [xd_done[0], xd_done[1]]
            xt_done_p = [0, 0]
            tok_done = [0, 0]
            for j in range(NT):
                par = j % 2
                xload[par] = p.emit(
                    "sp",
                    (lambda e, q=q, j=j, par=par: e.dma_start(
                        out=XT[:, par * 1024:(par + 1) * 1024],
                        in_=Xh[q * 1024 + j * 128: q * 1024 + (j + 1) * 128, :])),
                    waits=[("dve", xt_done[par]), ("pool", xt_done_p[par]),
                           ("act", xa_done[par])],
                    inc=("xd%d" % par, 16))
                apply_v = p.emit(
                    "dve",
                    (lambda v, j=j, par=par: v.tensor_scalar(
                        XTOK[par][:],
                        XT[:, par * 1024:(par + 1) * 1024],
                        ST[:, 16 + j:17 + j], ST[:, 40 + j:41 + j],
                        ALU.subtract, ALU.mult)),
                    waits=[("xd%d" % par, xload[par]), ("act", rstd_act),
                           ("pe", tok_done[par])],
                    inc=("dve", 1))
                xt_done[par] = apply_v
                xt_done_p[par] = p.emit(
                    "pool",
                    (lambda g, j=j, par=par: g.tensor_copy(
                        Y[:, j * 1024:(j + 1) * 1024],
                        XT[:, par * 1024:(par + 1) * 1024])),
                    waits=[("xd%d" % par, xload[par]),
                           ("stq", 128 * q)],
                    inc=("pool", 1))
                for dc in range(NDC):
                    wl = [("dve", apply_v)]
                    if dc == 0:
                        wl += [("act", pc_rel[0]), ("wconst", consts_w)]
                    mi = p.emit(
                        "pe",
                        (lambda t, dc=dc, par=par: t.transpose(
                            pC[:, dc * 128:(dc + 1) * 128],
                            XTOK[par][:, dc * 128:(dc + 1) * 128],
                            IDN[:])),
                        waits=wl, inc=("pe", 1) if dc == NDC - 1 else None)
                tok_done[par] = mi
                pc_rel[0] = p.emit(
                    "act",
                    (lambda s, j=j: s.activation(
                        XN.rearrange("p (dc t) -> p dc t", dc=NDC)[:, :, j * 128:(j + 1) * 128],
                        pC.rearrange("p (dc t) -> p dc t", dc=NDC),
                        AF.Copy)),
                    waits=[("pe", tok_done[par])], inc=("act", 1))
            ln_done_act = pc_rel[0]

            for j in range(NT if "r" in phases else 0):
                for dc in range(NDC):
                    wl = [("act", ln_done_act), ("act", lg_rel[0])] if dc == 0 else ()
                    mms = p.emit(
                        "pe",
                        (lambda t, j=j, dc=dc: t.matmul(
                            pD[:, 0:8],
                            XN[:, dc * 1024 + j * 128: dc * 1024 + (j + 1) * 128],
                            RW[:, dc * 8:(dc + 1) * 8],
                            start=(dc == 0), stop=(dc == NDC - 1))),
                        waits=wl, inc=("pe", 1) if dc == NDC - 1 else None)
                lg_rel[0] = p.emit(
                    "act",
                    lambda s: s.activation(RTR[:, 0:8], pD[:, 0:8], AF.Copy),
                    waits=[("pe", mms)], inc=("act", 1))
                p.emit("dve", lambda v: v.tensor_tensor(
                    RTR[:, 0:8], RTR[:, 0:8], BBs[:], ALU.add),
                    waits=[("act", lg_rel[0])], inc=("dve", 1))
                p.emit("dve", lambda v: v.reduce_max(RTR[:, 8:9], RTR[:, 0:8], AX.X), inc=("dve", 1))
                p.emit("dve", lambda v: v.tensor_scalar(
                    RTR[:, 16:24], RTR[:, 0:8], RTR[:, 8:9], None, ALU.is_equal), inc=("dve", 1))
                p.emit("dve", lambda v: v.tensor_tensor(
                    RTR[:, 16:24], RTR[:, 16:24], IOs[:, 8:16], ALU.mult), inc=("dve", 1))
                p.emit("dve", lambda v: v.tensor_scalar(
                    RTR[:, 16:24], RTR[:, 16:24], -1.0, 8.0, ALU.mult, ALU.add), inc=("dve", 1))
                p.emit("dve", lambda v: v.tensor_reduce(
                    RTR[:, 9:10], RTR[:, 16:24], AX.X, ALU.min), inc=("dve", 1))
                p.emit("dve", lambda v: v.tensor_scalar(
                    RTR[:, 24:32], IOs[:, 0:8], RTR[:, 9:10], None, ALU.is_equal), inc=("dve", 1))
                p.emit("dve", lambda v: v.scalar_tensor_tensor(
                    RTR[:, 32:40], RTR[:, 24:32], NEGBIG, RTR[:, 0:8], ALU.mult, ALU.add), inc=("dve", 1))
                p.emit("dve", lambda v: v.reduce_max(RTR[:, 10:11], RTR[:, 32:40], AX.X), inc=("dve", 1))
                p.emit("dve", lambda v: v.tensor_scalar(
                    RTR[:, 40:48], RTR[:, 32:40], RTR[:, 10:11], None, ALU.is_equal), inc=("dve", 1))
                p.emit("dve", lambda v: v.tensor_tensor(
                    RTR[:, 40:48], RTR[:, 40:48], IOs[:, 8:16], ALU.mult), inc=("dve", 1))
                p.emit("dve", lambda v: v.tensor_scalar(
                    RTR[:, 40:48], RTR[:, 40:48], -1.0, 8.0, ALU.mult, ALU.add), inc=("dve", 1))
                p.emit("dve", lambda v: v.tensor_reduce(
                    RTR[:, 11:12], RTR[:, 40:48], AX.X, ALU.min), inc=("dve", 1))
                p.emit("dve", lambda v: v.tensor_scalar(
                    RTR[:, 40:48], IOs[:, 0:8], RTR[:, 11:12], None, ALU.is_equal), inc=("dve", 1))
                dv_ = p.emit("dve", lambda v: v.tensor_tensor(
                    RTR[:, 12:13], RTR[:, 10:11], RTR[:, 8:9], ALU.subtract), inc=("dve", 1))
                e_act = p.emit("act", lambda s: s.activation(
                    RTR[:, 13:14], RTR[:, 12:13], AF.Exp),
                    waits=[("dve", dv_)], inc=("act", 1))
                p.emit("dve", lambda v: v.tensor_scalar(
                    RTR[:, 14:15], RTR[:, 13:14], 1.0, None, ALU.add),
                    waits=[("act", e_act)], inc=("dve", 1))
                rec_v = p.emit("dve", lambda v: v.reciprocal(RTR[:, 15:16], RTR[:, 14:15]),
                               inc=("dve", 1))
                p.emit("dve", lambda v: v.tensor_scalar(
                    RTR[:, 48:56], RTR[:, 24:32], RTR[:, 15:16], None, ALU.mult),
                    waits=[("dve", rec_v)], inc=("dve", 1))
                p.emit("dve", lambda v: v.tensor_tensor(
                    RTR[:, 13:14], RTR[:, 13:14], RTR[:, 15:16], ALU.mult), inc=("dve", 1))
                p.emit("dve", (lambda v, j=j: v.scalar_tensor_tensor(
                    CMB[:, j * 8:(j + 1) * 8], RTR[:, 40:48], RTR[:, 13:14],
                    RTR[:, 48:56], ALU.mult, ALU.add)), inc=("dve", 1))

            for g in range(NG if "a" in phases else 0):
                bq = load_wa(0)
                bk = load_wa(1)
                if pending_carry_pe[0] is not None:
                    pe_now = pending_carry_pe[0]
                    for dkc in range(NDC):
                        kcar_done[0] = p.emit(
                            "pool",
                            (lambda g_, dkc=dkc: g_.tensor_copy(
                                Kc[:, dkc * 128:(dkc + 1) * 128],
                                Kb[:, dkc * 512 + 384: dkc * 512 + 512])),
                            waits=[("pe", pe_now)], inc=("pool", 1))
                    vcar_done[0] = p.emit(
                        "pool",
                        lambda g_: g_.tensor_copy(Vc[:], Vb[:, 3 * 1024:4 * 1024]),
                        inc=("pool", 1))
                    pending_carry_pe[0] = None
                for dqc in range(NDC):
                    par = ef_par[0]
                    bank = (pEF, pGH)[par]
                    for dc in range(NDC):
                        wl = [("wa%d" % bq, wa_ready[bq]), ("act", ln_done_act),
                              ("dve", ef_rel[par])] if dc == 0 else ()
                        mi = p.emit(
                            "pe",
                            (lambda t, bank=bank, dqc=dqc, dc=dc, g=g, bq=bq: t.matmul(
                                bank[:, 0:512],
                                WA[:, bq * 8192 + dc * 1024 + dqc * 128:
                                   bq * 8192 + dc * 1024 + (dqc + 1) * 128],
                                XN[:, dc * 1024 + g * 512: dc * 1024 + (g + 1) * 512],
                                start=(dc == 0), stop=(dc == NDC - 1))),
                            waits=wl, inc=("pe", 1) if dc == NDC - 1 else None)
                    ef_rel[par] = p.emit(
                        "dve",
                        (lambda v, bank=bank, dqc=dqc: v.tensor_copy(
                            Qb[:, dqc * 512:(dqc + 1) * 512], bank[:, 0:512])),
                        waits=[("pe", mi)], inc=("dve", 1))
                    ef_par[0] ^= 1
                wa_last_use[bq] = p.now("pe")
                q_done = p.now("dve")
                for dkc in range(NDC):
                    par = ef_par[0]
                    bank = (pEF, pGH)[par]
                    for dc in range(NDC):
                        wl = [("wa%d" % bk, wa_ready[bk]), ("dve", ef_rel[par]),
                              ("pool", kcar_done[0])] if dc == 0 else ()
                        mi = p.emit(
                            "pe",
                            (lambda t, bank=bank, dkc=dkc, dc=dc, g=g, bk=bk: t.matmul(
                                bank[:, 0:512],
                                WA[:, bk * 8192 + dc * 1024 + dkc * 128:
                                   bk * 8192 + dc * 1024 + (dkc + 1) * 128],
                                XN[:, dc * 1024 + g * 512: dc * 1024 + (g + 1) * 512],
                                start=(dc == 0), stop=(dc == NDC - 1))),
                            waits=wl, inc=("pe", 1) if dc == NDC - 1 else None)
                    ef_rel[par] = p.emit(
                        "dve",
                        (lambda v, bank=bank, dkc=dkc: v.tensor_copy(
                            Kb[:, dkc * 512:(dkc + 1) * 512], bank[:, 0:512])),
                        waits=[("pe", mi)], inc=("dve", 1))
                    ef_par[0] ^= 1
                bv = load_wa(2)
                wa_last_use[bk] = p.now("pe")
                k_done = p.now("dve")
                for tl in range(4):
                    par = ef_par[0]
                    bank = (pEF, pGH)[par]
                    for half in range(2):
                        for dc in range(NDC):
                            wl = [("wa%d" % bv, wa_ready[bv]), ("dve", ef_rel[par]),
                                  ("pool", vcar_done[0])] if (dc == 0 and half == 0) else ()
                            mi = p.emit(
                                "pe",
                                (lambda t, bank=bank, tl=tl, dc=dc, g=g, bv=bv, half=half: t.matmul(
                                    bank[:, half * 512:(half + 1) * 512],
                                    XN[:, dc * 1024 + g * 512 + tl * 128:
                                       dc * 1024 + g * 512 + (tl + 1) * 128],
                                    WA[:, bv * 8192 + dc * 1024 + half * 512:
                                       bv * 8192 + dc * 1024 + (half + 1) * 512],
                                    start=(dc == 0), stop=(dc == NDC - 1))),
                                waits=wl,
                                inc=("pe", 1) if (dc == NDC - 1 and half == 1) else None)
                    ef_rel[par] = p.emit(
                        "dve",
                        (lambda v, bank=bank, tl=tl: v.tensor_copy(
                            Vb[:, tl * 1024:(tl + 1) * 1024], bank[:, 0:1024])),
                        waits=[("pe", mi)], inc=("dve", 1))
                    ef_par[0] ^= 1
                wa_last_use[bv] = p.now("pe")
                v_done = p.now("dve")

                if q == 0 and g == 0:
                    load_f(0, 0)
                    load_f(0, 1)

                for cl in range(4 if alvl >= 2 else 0):
                    c = g * 4 + cl
                    gq0 = (q == 0 and c == 0)
                    mask_ap = MKs[:, 256:512] if gq0 else MKs[:, 0:256]
                    pt_evac = 0
                    for hgrp in range(4):
                        pn_ready = {}
                        for hh in range(4):
                            h = hgrp * 4 + hh
                            bidx = hh % 2
                            sbank = (pA, pB)[bidx]
                            rowb = (h % 2) * 64
                            dkc = h // 2
                            wl = [("dve", k_done), ("act", sc_rel[bidx])]
                            if cl == 0:
                                p.emit(
                                    "pe",
                                    (lambda t, sbank=sbank, rowb=rowb, dkc=dkc, cl=cl: t.matmul(
                                        sbank[:, 0:128],
                                        Qb[rowb:rowb + 64, dkc * 512 + cl * 128: dkc * 512 + (cl + 1) * 128],
                                        Kc[rowb:rowb + 64, dkc * 128:(dkc + 1) * 128],
                                        start=True, stop=True)),
                                    waits=wl)
                                mi = p.emit(
                                    "pe",
                                    (lambda t, sbank=sbank, rowb=rowb, dkc=dkc, cl=cl: t.matmul(
                                        sbank[:, 128:256],
                                        Qb[rowb:rowb + 64, dkc * 512 + cl * 128: dkc * 512 + (cl + 1) * 128],
                                        Kb[rowb:rowb + 64, dkc * 512: dkc * 512 + 128],
                                        start=True, stop=True)),
                                    inc=("pe", 1))
                            else:
                                mi = p.emit(
                                    "pe",
                                    (lambda t, sbank=sbank, rowb=rowb, dkc=dkc, cl=cl: t.matmul(
                                        sbank[:, 0:256],
                                        Qb[rowb:rowb + 64, dkc * 512 + cl * 128: dkc * 512 + (cl + 1) * 128],
                                        Kb[rowb:rowb + 64, dkc * 512 + (cl - 1) * 128: dkc * 512 + (cl + 1) * 128],
                                        start=True, stop=True)),
                                    waits=wl, inc=("pe", 1))
                            ep = hh % 2
                            sc_rel[bidx] = p.emit(
                                "act",
                                (lambda s, sbank=sbank, ep=ep: s.activation(
                                    PEx[:, ep * 256:(ep + 1) * 256],
                                    sbank[:, 0:256],
                                    AF.Exp, bias=CST[:, 1:2], scale=0.125)),
                                waits=[("pe", mi), ("dve", pex_rel[ep])],
                                inc=("act", 1))
                            pex_rel[ep] = p.emit(
                                "dve",
                                (lambda v, ep=ep, hh=hh, mask_ap=mask_ap: v.scalar_tensor_tensor(
                                    PMs[:, hh * 256:(hh + 1) * 256],
                                    PEx[:, ep * 256:(ep + 1) * 256], 1.0, mask_ap,
                                    ALU.mult, ALU.mult,
                                    accum_out=RS[:, hh:hh + 1])),
                                waits=[("act", sc_rel[bidx])], inc=("dve", 1))
                        rec_v = p.emit("dve", lambda v: v.reciprocal(RD[:, 0:4], RS[:, 0:4]),
                                       inc=("dve", 1))
                        for hh in range(4):
                            pn_ready[hh] = p.emit(
                                "dve",
                                (lambda v, hh=hh: v.tensor_scalar(
                                    PMs[:, hh * 256:(hh + 1) * 256],
                                    PMs[:, hh * 256:(hh + 1) * 256],
                                    RD[:, hh:hh + 1], None, ALU.mult)),
                                waits=[("dve", rec_v)], inc=("dve", 1))
                        for hh in range(4 if alvl >= 3 else 0):
                            for kk in range(2):
                                wl = [("dve", pn_ready[hh])]
                                if hh == 0 and kk == 0:
                                    wl.append(("act", pc_rel[0]))
                                mi = p.emit(
                                    "pe",
                                    (lambda t, hh=hh, kk=kk: t.transpose(
                                        pC[:, (hh * 2 + kk) * 128:(hh * 2 + kk + 1) * 128],
                                        PMs[:, hh * 256 + kk * 128: hh * 256 + (kk + 1) * 128],
                                        IDN[:])),
                                    waits=wl,
                                    inc=("pe", 1) if (hh == 3 and kk == 1) else None)
                        if alvl >= 3:
                            pc_rel[0] = pt_evac = p.emit(
                                "act",
                                lambda s: s.activation(PTs[:], pC[:], AF.Copy),
                                waits=[("pe", mi)], inc=("act", 1))
                        o_mi = {}
                        for hh in range(4 if alvl >= 4 else 0):
                            h = hgrp * 4 + hh
                            if cl == 0:
                                lhs_prev = Vc[:, h * 64:(h + 1) * 64]
                            else:
                                lhs_prev = Vb[:, (cl - 1) * 1024 + h * 64:
                                              (cl - 1) * 1024 + (h + 1) * 64]
                            p.emit(
                                "pe",
                                (lambda t, lhs_prev=lhs_prev, hh=hh: t.matmul(
                                    pD[0:64, hh * 128:(hh + 1) * 128],
                                    lhs_prev, PTs[:, (hh * 2) * 128:(hh * 2 + 1) * 128],
                                    start=True, stop=False)),
                                waits=([("act", pt_evac), ("dve", v_done),
                                        ("act", af_rel[0])] if hh == 0 else
                                       [("act", pt_evac), ("dve", v_done)]))
                            o_mi[hh] = p.emit(
                                "pe",
                                (lambda t, cl=cl, h=h, hh=hh: t.matmul(
                                    pD[0:64, hh * 128:(hh + 1) * 128],
                                    Vb[:, cl * 1024 + h * 64: cl * 1024 + (h + 1) * 64],
                                    PTs[:, (hh * 2 + 1) * 128:(hh * 2 + 2) * 128],
                                    start=False, stop=True)),
                                inc=("pe", 1))
                        for hh in range(4 if alvl >= 4 else 0):
                            h = hgrp * 4 + hh
                            rowb = (h % 2) * 64
                            dc = h // 2
                            af_last = p.emit(
                                "act",
                                (lambda s, rowb=rowb, dc=dc, hh=hh: s.activation(
                                    AFb[rowb:rowb + 64, dc * 128:(dc + 1) * 128],
                                    pD[0:64, hh * 128:(hh + 1) * 128], AF.Copy)),
                                waits=[("pe", o_mi[3])], inc=("act", 1))
                            af_rel[0] = af_last
                    if alvl < 5:
                        continue
                    par = ef_par[0]
                    wo_bank = (pEF, pGH)[par]
                    for dc in range(NDC):
                        for half in range(2):
                            wl = [("act", af_last), ("wconst", consts_w),
                                  ("dve", ef_rel[par])] if (dc == 0 and half == 0) else ()
                            mi = p.emit(
                                "pe",
                                (lambda t, wo_bank=wo_bank, dc=dc, half=half: t.matmul(
                                    wo_bank[:, half * 512:(half + 1) * 512],
                                    AFb[:, dc * 128:(dc + 1) * 128],
                                    WO[:, dc * 1024 + half * 512: dc * 1024 + (half + 1) * 512],
                                    start=(dc == 0), stop=(dc == NDC - 1))),
                                waits=wl,
                                inc=("pe", 1) if (dc == NDC - 1 and half == 1) else None)
                    ef_rel[par] = p.emit(
                        "dve",
                        (lambda v, wo_bank=wo_bank, c=c: v.scalar_tensor_tensor(
                            Y[:, c * 1024:(c + 1) * 1024], wo_bank[:, 0:1024], 1.0,
                            Y[:, c * 1024:(c + 1) * 1024], ALU.mult, ALU.add)),
                        waits=[("pe", mi), ("pool", max(xt_done_p))],
                        inc=("dve", 1))
                    ef_par[0] ^= 1
                pending_carry_pe[0] = p.now("pe")

            act_barrier = p.now("act")
            dve_barrier = p.now("dve")
            if "f" not in phases:
                for j in range(NT):
                    y_done[q][j] = p.now("pool")
            for e in range(NE if "f" in phases else 0):
                for hg in range(NHG):
                    idx = e * NHG + hg
                    fb = idx % 2
                    fw = f_ready[fb]
                    fwsem = "wf%d" % fb
                    hb_done_tt = [0, 0]
                    for tt in range(2):
                        hbb = tt
                        for hc in range(NHC):
                            for dc in range(NDC):
                                wl = [(fwsem, fw),
                                      ("act", max(h1_rel[0], sc_rel[0], act_barrier if (idx == 0 and tt == 0 and hc == 0) else 0))] if dc == 0 else ()
                                mi1 = p.emit(
                                    "pe",
                                    (lambda t, fb=fb, dc=dc, hc=hc, tt=tt: t.matmul(
                                        pA[:, 0:512],
                                        W13[:, fb * 8192 + dc * 1024 + hc * 128:
                                            fb * 8192 + dc * 1024 + (hc + 1) * 128],
                                        XN[:, dc * 1024 + tt * 512: dc * 1024 + (tt + 1) * 512],
                                        start=(dc == 0), stop=(dc == NDC - 1))),
                                    waits=wl, inc=("pe", 1) if dc == NDC - 1 else None)
                            for dc in range(NDC):
                                wl = [("dve", h3_rel[0]),
                                      ("act", sc_rel[1] if (idx == 0 and tt == 0 and hc == 0) else 0)] if dc == 0 else ()
                                mi3 = p.emit(
                                    "pe",
                                    (lambda t, fb=fb, dc=dc, hc=hc, tt=tt: t.matmul(
                                        pB[:, 0:512],
                                        W13[:, fb * 8192 + dc * 1024 + 512 + hc * 128:
                                            fb * 8192 + dc * 1024 + 512 + (hc + 1) * 128],
                                        XN[:, dc * 1024 + tt * 512: dc * 1024 + (tt + 1) * 512],
                                        start=(dc == 0), stop=(dc == NDC - 1))),
                                    waits=wl, inc=("pe", 1) if dc == NDC - 1 else None)
                            sp_ = hc % 2
                            h1_rel[0] = p.emit(
                                "act",
                                (lambda s, sp_=sp_: s.activation(
                                    SIL[:, sp_ * 512:(sp_ + 1) * 512], pA[:, 0:512], AF.Silu)),
                                waits=[("pe", mi1), ("dve", h3_rel[0])], inc=("act", 1))
                            h3_rel[0] = p.emit(
                                "dve",
                                (lambda v, sp_=sp_, hbb=hbb, hc=hc: v.tensor_tensor(
                                    HB[:, hbb * 2048 + hc * 512: hbb * 2048 + (hc + 1) * 512],
                                    SIL[:, sp_ * 512:(sp_ + 1) * 512], pB[:, 0:512], ALU.mult)),
                                waits=[("act", h1_rel[0]), ("pe", mi3)], inc=("dve", 1))
                        hb_done_tt[tt] = h3_rel[0]
                    for tt in range(2):
                        hbb = tt
                        for jj in range(4):
                            j = tt * 4 + jj
                            par = ef_par[0]
                            bank = (pEF, pGH)[par]
                            for hc in range(NHC):
                                for half in range(2):
                                    wl = [("dve", max(hb_done_tt[tt], ef_rel[par]))] if (hc == 0 and half == 0) else ()
                                    mi = p.emit(
                                        "pe",
                                        (lambda t, bank=bank, fb=fb, hbb=hbb, hc=hc, jj=jj, half=half: t.matmul(
                                            bank[:, half * 512:(half + 1) * 512],
                                            HB[:, hbb * 2048 + hc * 512 + jj * 128:
                                               hbb * 2048 + hc * 512 + (jj + 1) * 128],
                                            W2[:, fb * 4096 + hc * 1024 + half * 512:
                                               fb * 4096 + hc * 1024 + (half + 1) * 512],
                                            start=(hc == 0), stop=(hc == NHC - 1))),
                                        waits=wl,
                                        inc=("pe", 1) if (hc == NHC - 1 and half == 1) else None)
                            if e < 2:
                                ef_rel[par] = p.emit(
                                    "dve",
                                    (lambda v, bank=bank, j=j: v.scalar_tensor_tensor(
                                        Y[:, j * 1024:(j + 1) * 1024], bank[:, 0:1024], 1.0,
                                        Y[:, j * 1024:(j + 1) * 1024], ALU.mult, ALU.add)),
                                    waits=[("pe", mi)], inc=("dve", 1))
                            else:
                                ef_rel[par] = p.emit(
                                    "dve",
                                    (lambda v, bank=bank, j=j, e=e: v.scalar_tensor_tensor(
                                        Y[:, j * 1024:(j + 1) * 1024], bank[:, 0:1024],
                                        CMB[:, j * 8 + (e - 2): j * 8 + (e - 1)],
                                        Y[:, j * 1024:(j + 1) * 1024], ALU.mult, ALU.add)),
                                    waits=[("pe", mi)], inc=("dve", 1))
                            if e == NE - 1 and hg == NHG - 1:
                                y_done[q][j] = ef_rel[par]
                            ef_par[0] ^= 1
                    f_last_use[fb] = p.now("pe")
                    nidx = idx + 2
                    if nidx < NE * NHG:
                        ne_, nhg_ = divmod(nidx, NHG)
                        load_f(ne_, nhg_)
                    elif q < NQ - 1:
                        ne_, nhg_ = divmod(nidx - NE * NHG, NHG)
                        load_f(ne_, nhg_)

            sc_rel[0] = max(sc_rel[0], h1_rel[0])
            store_sem = "dve" if "f" in phases else "pool"
            for j in range(NT):
                store_done[q][j] = p.emit(
                    "sp",
                    (lambda e_, q=q, j=j: e_.dma_start(
                        out=Yh[q * 1024 + j * 128: q * 1024 + (j + 1) * 128, :],
                        in_=Y[:, j * 1024:(j + 1) * 1024])),
                    waits=[(store_sem, y_done[q][j])], inc=("stq", 16))

        p.wait_only("sp", [("stq", p.now("stq"))])

        def replay(lst):
            def f(eng):
                waited = {}
                for fn, waits, inc in lst:
                    for (s, v) in waits:
                        if waited.get(s, 0) < v:
                            eng.wait_ge(sems[s], int(v))
                            waited[s] = v
                    if fn is None:
                        continue
                    ins = fn(eng)
                    if inc is not None and ins is not None:
                        ins.then_inc(sems[inc[0]], inc[1])
            return f

        block = E(nc.Block())
        block.tensor(replay(p.ops["pe"]))
        block.scalar(replay(p.ops["act"]))
        block.vector(replay(p.ops["dve"]))
        block.gpsimd(replay(p.ops["pool"]))
        block.sync(replay(p.ops["sp"]))

    return nc



def _prep_weights(ins):
    f32 = np.float32

    def wt_blocks(WT):
        return np.ascontiguousarray(
            WT.reshape(8, 128, 1024).transpose(1, 0, 2).reshape(128, 8192))

    WAarr = np.ascontiguousarray(np.stack([
        wt_blocks(np.ascontiguousarray(ins["L0_attn_Wq"].T)),
        wt_blocks(np.ascontiguousarray(ins["L0_attn_Wk"].T)),
        wt_blocks(np.ascontiguousarray(ins["L0_attn_Wv"].T)),
    ])).astype(f32)
    WOarr = wt_blocks(np.ascontiguousarray(ins["L0_attn_Wo"].T)).astype(f32)

    w1 = np.concatenate([ins["L0_ffn_sw1"], ins["L0_ffn_w1"]], axis=0)
    w3 = np.concatenate([ins["L0_ffn_sw3"], ins["L0_ffn_w3"]], axis=0)
    w2 = np.concatenate([ins["L0_ffn_sw2"], ins["L0_ffn_w2"]], axis=0)

    w1T = np.transpose(w1, (0, 2, 1)).reshape(NE, 8, 128, 4, 512)
    w3T = np.transpose(w3, (0, 2, 1)).reshape(NE, 8, 128, 4, 512)
    a1 = np.transpose(w1T, (0, 3, 2, 1, 4))
    a3 = np.transpose(w3T, (0, 3, 2, 1, 4))
    W13arr = np.ascontiguousarray(
        np.concatenate([a1, a3], axis=4).reshape(NE, NHG, 128, 8192)).astype(f32)

    w2T = np.transpose(w2, (0, 2, 1)).reshape(NE, 4, 4, 128, 1024)
    W2arr = np.ascontiguousarray(
        np.transpose(w2T, (0, 1, 3, 2, 4)).reshape(NE, NHG, 128, 4096)).astype(f32)

    rwT = np.ascontiguousarray(ins["L0_ffn_router_w"].T).reshape(8, 128, 8)
    RWarr = np.ascontiguousarray(rwT.transpose(1, 0, 2).reshape(128, 64)).astype(f32)

    BBarr = np.ascontiguousarray(
        np.tile(np.asarray(ins["L0_ffn_expert_bias"]).reshape(1, 8), (128, 1))).astype(f32)

    i = np.arange(128)[:, None]
    j = np.arange(256)[None, :]
    band = ((j >= i + 1) & (j <= i + 128)).astype(f32)
    first = (band * (j >= 128)).astype(f32)
    MKarr = np.ascontiguousarray(np.concatenate([band, first], axis=1)).astype(f32)

    io = np.tile(np.arange(8, dtype=f32).reshape(1, 8), (128, 1))
    IOarr = np.ascontiguousarray(np.concatenate([io, 8.0 - io], axis=1)).astype(f32)

    IDarr = np.eye(128, dtype=f32)
    CSarr = np.ascontiguousarray(
        np.tile(np.asarray([[1e-5, EXP_SHIFT]], dtype=f32), (128, 1)))

    return dict(wa=WAarr, wo=WOarr, wf13=W13arr, wf2=W2arr, rwt=RWarr,
                biasb=BBarr, masks=MKarr, iotas=IOarr, ident=IDarr, csts=CSarr)


_NC_CACHE = {}


def kernel(**inputs):
    x = np.asarray(inputs["L0_attn_x"], dtype=np.float32)
    shared = _prep_weights({k: np.asarray(v) for k, v in inputs.items()})

    if "main" not in _NC_CACHE:
        _NC_CACHE["main"] = build_nc()
    nc = _NC_CACHE["main"]

    in_maps = []
    for c in range(8):
        m = dict(shared)
        m["x"] = np.ascontiguousarray(x[c])
        in_maps.append(m)

    res = run_bass_kernel_spmd(nc, in_maps, core_ids=list(range(8)))
    out = np.stack([res.results[c]["y"] for c in range(8)], axis=0)
    return out.astype(np.float32)


if __name__ == "__main__":
    build_nc()
    print("built ok")


# version 03
# speedup vs baseline: 1.0151x; 1.0026x over previous
import numpy as np
from contextlib import ExitStack

import concourse.bass as bass
import concourse.mybir as mybir
from concourse.bass_utils import run_bass_kernel_spmd

F32 = mybir.dt.float32
BF16 = mybir.dt.bfloat16
AF = mybir.ActivationFunctionType
ALU = mybir.AluOpType
AX = mybir.AxisListType

SEQ, D = 4096, 1024
NQ, TQ, NT = 4, 1024, 8
NG = 2
NE, NHG, NHC = 10, 4, 4
NDC = 8
EXP_SHIFT = -2.0
NEGBIG = -1e30

ENGINES = ("pe", "act", "dve", "pool", "sp")


class Prog:

    def __init__(self):
        self.ops = {e: [] for e in ENGINES}
        self.cnt = {}

    def emit(self, eng, fn, waits=(), inc=None):
        waits = list(waits)
        if eng in ("act", "dve"):
            if inc is None:
                inc = (eng, 1)
            prev = self.cnt.get(eng, 0)
            if prev:
                waits.append((eng, prev))
        elif eng == "pool" and inc is not None and inc[0] == "pool":
            prev = self.cnt.get("pool", 0)
            if prev:
                waits.append(("pool", prev))
        val = None
        if inc is not None:
            s, a = inc
            self.cnt[s] = self.cnt.get(s, 0) + a
            val = self.cnt[s]
        self.ops[eng].append((fn, [(s, v) for (s, v) in waits if v > 0], inc))
        return val

    def wait_only(self, eng, waits):
        self.ops[eng].append((None, [(s, v) for (s, v) in waits if v > 0], None))

    def now(self, sem):
        return self.cnt.get(sem, 0)


def build_nc(phases="lraf"):
    alvl = 5
    import re as _re
    _m = _re.search(r"a(\d)", phases)
    if _m:
        alvl = int(_m.group(1))
    nc = bass.Bass("TRN2", target_bir_lowering=False)

    Xh = nc.dram_tensor("x", [SEQ, D], F32, kind="ExternalInput").ap()
    WAh = nc.dram_tensor("wa", [3, 128, 8192], F32, kind="ExternalInput").ap()
    WOh = nc.dram_tensor("wo", [128, 8192], F32, kind="ExternalInput").ap()
    W13h = nc.dram_tensor("wf13", [NE, NHG, 128, 8192], F32, kind="ExternalInput").ap()
    W2h = nc.dram_tensor("wf2", [NE, NHG, 128, 4096], F32, kind="ExternalInput").ap()
    RWh = nc.dram_tensor("rwt", [128, 64], F32, kind="ExternalInput").ap()
    BBh = nc.dram_tensor("biasb", [128, 8], F32, kind="ExternalInput").ap()
    MKh = nc.dram_tensor("masks", [128, 1024], F32, kind="ExternalInput").ap()
    IOh = nc.dram_tensor("iotas", [128, 16], F32, kind="ExternalInput").ap()
    IDh = nc.dram_tensor("ident", [128, 128], F32, kind="ExternalInput").ap()
    CSh = nc.dram_tensor("csts", [128, 2], F32, kind="ExternalInput").ap()
    Yh = nc.dram_tensor("y", [SEQ, D], F32, kind="ExternalOutput").ap()

    p = Prog()

    with ExitStack() as ctx:
        E = ctx.enter_context
        sb = lambda name, cols, dt: E(nc.sbuf_tensor(name, [128, cols], dt))

        XN = sb("XN", NDC * 1024, BF16)
        Y = sb("Y", NT * 1024, F32)
        W13 = sb("W13", 2 * 8192, BF16)
        W2 = sb("W2", 2 * 4096, BF16)
        WA = sb("WA", 2 * 8192, BF16)
        WO = sb("WO", 8192, BF16)
        RW = sb("RW", 64, BF16)
        Qb = sb("Qb", NDC * 512, BF16)
        Kb = sb("Kb", NDC * 512, BF16)
        Kc = sb("Kc", NDC * 128, BF16)
        Vb = sb("Vb", 4 * 1024, BF16)
        Vc = sb("Vc", 1024, BF16)
        AFb = sb("AFb", NDC * 128, BF16)
        PEx = sb("PEx", 4 * 256, F32)
        PTs = sb("PTs", 8 * 128, BF16)
        PMs = sb("PMs", 4 * 256, BF16)
        RS = sb("RS", 8, F32)
        RD = sb("RD", 8, F32)
        HB = sb("HB", 2 * 2048, BF16)
        SIL = sb("SIL", 2 * 512, F32)
        XT = sb("XT", 2 * 1024, F32)
        ST = sb("ST", 48, F32)
        RTR = sb("RTR", 64, F32)
        CMB = sb("CMB", NT * 8, F32)
        IDN = sb("IDN", 128, BF16)
        MKs = sb("MKs", 1024, BF16)
        IOs = sb("IOs", 16, F32)
        BBs = sb("BBs", 8, F32)
        CST = sb("CST", 2, F32)

        SQ = SIL
        XTOK = (PMs, PTs)

        pA = E(nc.psum_tensor("pA", [128, 512], F32))
        pB = E(nc.psum_tensor("pB", [128, 512], F32))
        pC = E(nc.psum_tensor("pC", [128, 1024], BF16))
        pD = E(nc.psum_tensor("pD", [128, 512], F32))
        pEF = E(nc.psum_tensor("pEF", [128, 1024], F32))
        pGH = E(nc.psum_tensor("pGH", [128, 1024], F32))

        sems = {n: E(nc.semaphore(n)) for n in
                ("wconst", "wa0", "wa1", "wf0", "wf1", "xd0", "xd1",
                 "stq", "pe", "act", "dve", "pool")}

        pc_rel = [0]
        sc_rel = {0: 0, 1: 0}
        h1_rel = [0]
        h3_rel = [0]
        ef_rel = {0: 0, 1: 0}
        ef_par = [0]
        pex_rel = {0: 0, 1: 0, 2: 0, 3: 0}
        lg_rel = [0]
        af_rel2 = {0: 0, 1: 0}
        t_done2 = {0: 0, 1: 0}
        pc_rel2 = {0: ("dve", 0), 1: ("dve", 0)}

        p.emit("pool", lambda g: g.dma_start(out=IDN[:], in_=IDh), inc=("wconst", 16))
        p.emit("pool", lambda g: g.dma_start(out=MKs[:], in_=MKh), inc=("wconst", 16))
        p.emit("pool", lambda g: g.dma_start(out=IOs[:], in_=IOh), inc=("wconst", 16))
        p.emit("pool", lambda g: g.dma_start(out=BBs[:], in_=BBh), inc=("wconst", 16))
        p.emit("pool", lambda g: g.dma_start(out=RW[:], in_=RWh), inc=("wconst", 16))
        p.emit("pool", lambda g: g.dma_start(out=WO[:], in_=WOh), inc=("wconst", 16))
        p.emit("pool", lambda g: g.dma_start(out=CST[:], in_=CSh), inc=("wconst", 16))
        consts_w = p.now("wconst")
        p.emit("dve", lambda v: v.memset(Kc[:], 0), inc=("dve", 1))
        p.emit("dve", lambda v: v.memset(Vc[:], 0), inc=("dve", 1))

        wa_seq = [0]
        wa_last_use = [0, 0]
        wa_ready = [0, 0]

        def load_wa(widx):
            b = wa_seq[0] % 2
            wv = p.emit("pool",
                        (lambda g, b=b, widx=widx: g.dma_start(
                            out=WA[:, b * 8192:(b + 1) * 8192], in_=WAh[widx])),
                        waits=[("pe", wa_last_use[b])], inc=("wa%d" % b, 16))
            wa_ready[b] = wv
            wa_seq[0] += 1
            return b

        f_seq = [0]
        f_last_use = [0, 0]
        f_ready = [0, 0]

        def load_f(e, hg):
            b = f_seq[0] % 2
            p.emit("pool",
                   (lambda g, b=b, e=e, hg=hg: g.dma_start(
                       out=W13[:, b * 8192:(b + 1) * 8192], in_=W13h[e, hg])),
                   waits=[("pe", f_last_use[b])], inc=("wf%d" % b, 16))
            wv = p.emit("pool",
                        (lambda g, b=b, e=e, hg=hg: g.dma_start(
                            out=W2[:, b * 4096:(b + 1) * 4096], in_=W2h[e, hg])),
                        inc=("wf%d" % b, 16))
            f_ready[b] = wv
            f_seq[0] += 1

        store_done = np.zeros((NQ, NT), dtype=np.int64)
        pending_carry_pe = [None]
        kcar_done = [0]
        vcar_done = [0]
        y_done = np.zeros((NQ, NT), dtype=np.int64)

        for q in range(NQ):
            xa_done = [0, 0]
            xd_done = [0, 0]
            xload = [0, 0]
            for j in range(NT):
                par = j % 2
                xload[par] = p.emit(
                    "sp",
                    (lambda e, q=q, j=j, par=par: e.dma_start(
                        out=XT[:, par * 1024:(par + 1) * 1024],
                        in_=Xh[q * 1024 + j * 128: q * 1024 + (j + 1) * 128, :])),
                    waits=[("act", xa_done[par]), ("dve", xd_done[par])],
                    inc=("xd%d" % par, 16))
                sq_wl = [("xd%d" % par, xload[par])]
                if j == 0:
                    sq_wl.append(("dve", h3_rel[0]))
                xa_done[par] = p.emit(
                    "act",
                    (lambda s, j=j, par=par: s.activation(
                        SQ[:], XT[:, par * 1024:(par + 1) * 1024],
                        AF.Square, accum_out=ST[:, 8 + j:9 + j])),
                    waits=sq_wl, inc=("act", 1))
                xd_done[par] = p.emit(
                    "dve",
                    (lambda v, j=j, par=par: v.reduce_sum(
                        ST[:, j:j + 1], XT[:, par * 1024:(par + 1) * 1024], AX.X)),
                    waits=[("xd%d" % par, xload[par])], inc=("dve", 1))
            p.emit("dve", lambda v: v.tensor_scalar(
                ST[:, 16:24], ST[:, 0:8], 1.0 / D, None, ALU.mult),
                waits=[("act", xa_done[0]), ("act", xa_done[1])], inc=("dve", 1))
            p.emit("dve", lambda v: v.tensor_tensor(
                ST[:, 24:32], ST[:, 16:24], ST[:, 16:24], ALU.mult), inc=("dve", 1))
            p.emit("dve", lambda v: v.tensor_scalar(
                ST[:, 32:40], ST[:, 8:16], 1.0 / D, None, ALU.mult), inc=("dve", 1))
            stats_dve = p.emit("dve", lambda v: v.tensor_tensor(
                ST[:, 32:40], ST[:, 32:40], ST[:, 24:32], ALU.subtract), inc=("dve", 1))
            p.emit("act", lambda s: s.activation(
                ST[:, 40:48], ST[:, 32:40], AF.Ln, bias=CST[:, 0:1], scale=1.0),
                waits=[("dve", stats_dve), ("wconst", consts_w)], inc=("act", 1))
            rstd_act = p.emit("act", lambda s: s.activation(
                ST[:, 40:48], ST[:, 40:48], AF.Exp, bias=0.0, scale=-0.5),
                inc=("act", 1))

            xt_done = [xd_done[0], xd_done[1]]
            xt_done_p = [0, 0]
            tok_done = [0, 0]
            for j in range(NT):
                par = j % 2
                xload[par] = p.emit(
                    "sp",
                    (lambda e, q=q, j=j, par=par: e.dma_start(
                        out=XT[:, par * 1024:(par + 1) * 1024],
                        in_=Xh[q * 1024 + j * 128: q * 1024 + (j + 1) * 128, :])),
                    waits=[("dve", xt_done[par]), ("pool", xt_done_p[par]),
                           ("act", xa_done[par])],
                    inc=("xd%d" % par, 16))
                apply_v = p.emit(
                    "dve",
                    (lambda v, j=j, par=par: v.tensor_scalar(
                        XTOK[par][:],
                        XT[:, par * 1024:(par + 1) * 1024],
                        ST[:, 16 + j:17 + j], ST[:, 40 + j:41 + j],
                        ALU.subtract, ALU.mult)),
                    waits=[("xd%d" % par, xload[par]), ("act", rstd_act),
                           ("pe", tok_done[par])],
                    inc=("dve", 1))
                xt_done[par] = apply_v
                xt_done_p[par] = p.emit(
                    "pool",
                    (lambda g, j=j, par=par: g.tensor_copy(
                        Y[:, j * 1024:(j + 1) * 1024],
                        XT[:, par * 1024:(par + 1) * 1024])),
                    waits=[("xd%d" % par, xload[par]),
                           ("stq", 128 * q)],
                    inc=("pool", 1))
                for dc in range(NDC):
                    wl = [("dve", apply_v)]
                    if dc == 0:
                        m0, m1 = pc_rel2[0], pc_rel2[1]
                        wl += [m0, m1, ("wconst", consts_w)]
                    mi = p.emit(
                        "pe",
                        (lambda t, dc=dc, par=par: t.transpose(
                            pC[:, dc * 128:(dc + 1) * 128],
                            XTOK[par][:, dc * 128:(dc + 1) * 128],
                            IDN[:])),
                        waits=wl, inc=("pe", 1) if dc == NDC - 1 else None)
                tok_done[par] = mi
                xn_ev = p.emit(
                    "dve",
                    (lambda v, j=j: v.tensor_copy(
                        XN.rearrange("p (dc t) -> p dc t", dc=NDC)[:, :, j * 128:(j + 1) * 128],
                        pC.rearrange("p (dc t) -> p dc t", dc=NDC))),
                    waits=[("pe", tok_done[par])], inc=("dve", 1))
                pc_rel2[0] = pc_rel2[1] = ("dve", xn_ev)
            ln_done = ("dve", max(pc_rel2[0][1], pc_rel2[1][1]))

            for j in range(NT if "r" in phases else 0):
                for dc in range(NDC):
                    wl = [ln_done, ("act", lg_rel[0])] if dc == 0 else ()
                    mms = p.emit(
                        "pe",
                        (lambda t, j=j, dc=dc: t.matmul(
                            pD[:, 0:8],
                            XN[:, dc * 1024 + j * 128: dc * 1024 + (j + 1) * 128],
                            RW[:, dc * 8:(dc + 1) * 8],
                            start=(dc == 0), stop=(dc == NDC - 1))),
                        waits=wl, inc=("pe", 1) if dc == NDC - 1 else None)
                lg_rel[0] = p.emit(
                    "act",
                    lambda s: s.activation(RTR[:, 0:8], pD[:, 0:8], AF.Copy),
                    waits=[("pe", mms)], inc=("act", 1))
                p.emit("dve", lambda v: v.tensor_tensor(
                    RTR[:, 0:8], RTR[:, 0:8], BBs[:], ALU.add),
                    waits=[("act", lg_rel[0])], inc=("dve", 1))
                p.emit("dve", lambda v: v.reduce_max(RTR[:, 8:9], RTR[:, 0:8], AX.X), inc=("dve", 1))
                p.emit("dve", lambda v: v.tensor_scalar(
                    RTR[:, 16:24], RTR[:, 0:8], RTR[:, 8:9], None, ALU.is_equal), inc=("dve", 1))
                p.emit("dve", lambda v: v.tensor_tensor(
                    RTR[:, 16:24], RTR[:, 16:24], IOs[:, 8:16], ALU.mult), inc=("dve", 1))
                p.emit("dve", lambda v: v.tensor_scalar(
                    RTR[:, 16:24], RTR[:, 16:24], -1.0, 8.0, ALU.mult, ALU.add), inc=("dve", 1))
                p.emit("dve", lambda v: v.tensor_reduce(
                    RTR[:, 9:10], RTR[:, 16:24], AX.X, ALU.min), inc=("dve", 1))
                p.emit("dve", lambda v: v.tensor_scalar(
                    RTR[:, 24:32], IOs[:, 0:8], RTR[:, 9:10], None, ALU.is_equal), inc=("dve", 1))
                p.emit("dve", lambda v: v.scalar_tensor_tensor(
                    RTR[:, 32:40], RTR[:, 24:32], NEGBIG, RTR[:, 0:8], ALU.mult, ALU.add), inc=("dve", 1))
                p.emit("dve", lambda v: v.reduce_max(RTR[:, 10:11], RTR[:, 32:40], AX.X), inc=("dve", 1))
                p.emit("dve", lambda v: v.tensor_scalar(
                    RTR[:, 40:48], RTR[:, 32:40], RTR[:, 10:11], None, ALU.is_equal), inc=("dve", 1))
                p.emit("dve", lambda v: v.tensor_tensor(
                    RTR[:, 40:48], RTR[:, 40:48], IOs[:, 8:16], ALU.mult), inc=("dve", 1))
                p.emit("dve", lambda v: v.tensor_scalar(
                    RTR[:, 40:48], RTR[:, 40:48], -1.0, 8.0, ALU.mult, ALU.add), inc=("dve", 1))
                p.emit("dve", lambda v: v.tensor_reduce(
                    RTR[:, 11:12], RTR[:, 40:48], AX.X, ALU.min), inc=("dve", 1))
                p.emit("dve", lambda v: v.tensor_scalar(
                    RTR[:, 40:48], IOs[:, 0:8], RTR[:, 11:12], None, ALU.is_equal), inc=("dve", 1))
                dv_ = p.emit("dve", lambda v: v.tensor_tensor(
                    RTR[:, 12:13], RTR[:, 10:11], RTR[:, 8:9], ALU.subtract), inc=("dve", 1))
                e_act = p.emit("act", lambda s: s.activation(
                    RTR[:, 13:14], RTR[:, 12:13], AF.Exp),
                    waits=[("dve", dv_)], inc=("act", 1))
                p.emit("dve", lambda v: v.tensor_scalar(
                    RTR[:, 14:15], RTR[:, 13:14], 1.0, None, ALU.add),
                    waits=[("act", e_act)], inc=("dve", 1))
                rec_v = p.emit("dve", lambda v: v.reciprocal(RTR[:, 15:16], RTR[:, 14:15]),
                               inc=("dve", 1))
                p.emit("dve", lambda v: v.tensor_scalar(
                    RTR[:, 48:56], RTR[:, 24:32], RTR[:, 15:16], None, ALU.mult),
                    waits=[("dve", rec_v)], inc=("dve", 1))
                p.emit("dve", lambda v: v.tensor_tensor(
                    RTR[:, 13:14], RTR[:, 13:14], RTR[:, 15:16], ALU.mult), inc=("dve", 1))
                p.emit("dve", (lambda v, j=j: v.scalar_tensor_tensor(
                    CMB[:, j * 8:(j + 1) * 8], RTR[:, 40:48], RTR[:, 13:14],
                    RTR[:, 48:56], ALU.mult, ALU.add)), inc=("dve", 1))

            for g in range(NG if "a" in phases else 0):
                bq = load_wa(0)
                bk = load_wa(1)
                if pending_carry_pe[0] is not None:
                    pe_now = pending_carry_pe[0]
                    for dkc in range(NDC):
                        kcar_done[0] = p.emit(
                            "pool",
                            (lambda g_, dkc=dkc: g_.tensor_copy(
                                Kc[:, dkc * 128:(dkc + 1) * 128],
                                Kb[:, dkc * 512 + 384: dkc * 512 + 512])),
                            waits=[("pe", pe_now)], inc=("pool", 1))
                    vcar_done[0] = p.emit(
                        "pool",
                        lambda g_: g_.tensor_copy(Vc[:], Vb[:, 3 * 1024:4 * 1024]),
                        inc=("pool", 1))
                    pending_carry_pe[0] = None
                for dqc in range(NDC):
                    par = ef_par[0]
                    bank = (pEF, pGH)[par]
                    for dc in range(NDC):
                        wl = [("wa%d" % bq, wa_ready[bq]), ln_done,
                              ("dve", ef_rel[par])] if dc == 0 else ()
                        mi = p.emit(
                            "pe",
                            (lambda t, bank=bank, dqc=dqc, dc=dc, g=g, bq=bq: t.matmul(
                                bank[:, 0:512],
                                WA[:, bq * 8192 + dc * 1024 + dqc * 128:
                                   bq * 8192 + dc * 1024 + (dqc + 1) * 128],
                                XN[:, dc * 1024 + g * 512: dc * 1024 + (g + 1) * 512],
                                start=(dc == 0), stop=(dc == NDC - 1))),
                            waits=wl, inc=("pe", 1) if dc == NDC - 1 else None)
                    ef_rel[par] = p.emit(
                        "dve",
                        (lambda v, bank=bank, dqc=dqc: v.tensor_copy(
                            Qb[:, dqc * 512:(dqc + 1) * 512], bank[:, 0:512])),
                        waits=[("pe", mi)], inc=("dve", 1))
                    ef_par[0] ^= 1
                wa_last_use[bq] = p.now("pe")
                q_done = p.now("dve")
                for dkc in range(NDC):
                    par = ef_par[0]
                    bank = (pEF, pGH)[par]
                    for dc in range(NDC):
                        wl = [("wa%d" % bk, wa_ready[bk]), ("dve", ef_rel[par]),
                              ("pool", kcar_done[0])] if dc == 0 else ()
                        mi = p.emit(
                            "pe",
                            (lambda t, bank=bank, dkc=dkc, dc=dc, g=g, bk=bk: t.matmul(
                                bank[:, 0:512],
                                WA[:, bk * 8192 + dc * 1024 + dkc * 128:
                                   bk * 8192 + dc * 1024 + (dkc + 1) * 128],
                                XN[:, dc * 1024 + g * 512: dc * 1024 + (g + 1) * 512],
                                start=(dc == 0), stop=(dc == NDC - 1))),
                            waits=wl, inc=("pe", 1) if dc == NDC - 1 else None)
                    ef_rel[par] = p.emit(
                        "dve",
                        (lambda v, bank=bank, dkc=dkc: v.tensor_copy(
                            Kb[:, dkc * 512:(dkc + 1) * 512], bank[:, 0:512])),
                        waits=[("pe", mi)], inc=("dve", 1))
                    ef_par[0] ^= 1
                bv = load_wa(2)
                wa_last_use[bk] = p.now("pe")
                k_done = p.now("dve")
                for tl in range(4):
                    par = ef_par[0]
                    bank = (pEF, pGH)[par]
                    for half in range(2):
                        for dc in range(NDC):
                            wl = [("wa%d" % bv, wa_ready[bv]), ("dve", ef_rel[par]),
                                  ("pool", vcar_done[0])] if (dc == 0 and half == 0) else ()
                            mi = p.emit(
                                "pe",
                                (lambda t, bank=bank, tl=tl, dc=dc, g=g, bv=bv, half=half: t.matmul(
                                    bank[:, half * 512:(half + 1) * 512],
                                    XN[:, dc * 1024 + g * 512 + tl * 128:
                                       dc * 1024 + g * 512 + (tl + 1) * 128],
                                    WA[:, bv * 8192 + dc * 1024 + half * 512:
                                       bv * 8192 + dc * 1024 + (half + 1) * 512],
                                    start=(dc == 0), stop=(dc == NDC - 1))),
                                waits=wl,
                                inc=("pe", 1) if (dc == NDC - 1 and half == 1) else None)
                    ef_rel[par] = p.emit(
                        "dve",
                        (lambda v, bank=bank, tl=tl: v.tensor_copy(
                            Vb[:, tl * 1024:(tl + 1) * 1024], bank[:, 0:1024])),
                        waits=[("pe", mi)], inc=("dve", 1))
                    ef_par[0] ^= 1
                wa_last_use[bv] = p.now("pe")
                v_done = p.now("dve")

                if q == 0 and g == 0:
                    load_f(0, 0)
                    load_f(0, 1)

                for cl in range(4 if alvl >= 2 else 0):
                    c = g * 4 + cl
                    gq0 = (q == 0 and c == 0)
                    moff = 512 if gq0 else 0

                    def s_group(g2, cl=cl, moff=moff, k_done=k_done):
                        b = g2 % 2
                        sbank = (pA, pB)[b]
                        wl = [("dve", k_done), ("act", sc_rel[b])]
                        for hh in range(2):
                            h = 4 * (g2 // 2) + (g2 % 2) + 2 * hh
                            rowb = (g2 % 2) * 64
                            dkc = h // 2
                            first = (hh == 0)
                            if cl == 0:
                                p.emit(
                                    "pe",
                                    (lambda t, sbank=sbank, hh=hh, rowb=rowb, dkc=dkc, cl=cl, first=first: t.matmul(
                                        sbank[:, hh * 256: hh * 256 + 128],
                                        Qb[rowb:rowb + 64, dkc * 512 + cl * 128: dkc * 512 + (cl + 1) * 128],
                                        Kc[rowb:rowb + 64, dkc * 128:(dkc + 1) * 128],
                                        start=first, stop=False)),
                                    waits=wl if first else ())
                                p.emit(
                                    "pe",
                                    (lambda t, sbank=sbank, hh=hh, rowb=rowb, dkc=dkc, cl=cl: t.matmul(
                                        sbank[:, hh * 256 + 128: hh * 256 + 256],
                                        Qb[rowb:rowb + 64, dkc * 512 + cl * 128: dkc * 512 + (cl + 1) * 128],
                                        Kb[rowb:rowb + 64, dkc * 512: dkc * 512 + 128],
                                        start=False, stop=False)))
                            else:
                                p.emit(
                                    "pe",
                                    (lambda t, sbank=sbank, hh=hh, rowb=rowb, dkc=dkc, cl=cl, first=first: t.matmul(
                                        sbank[:, hh * 256: hh * 256 + 256],
                                        Qb[rowb:rowb + 64, dkc * 512 + cl * 128: dkc * 512 + (cl + 1) * 128],
                                        Kb[rowb:rowb + 64, dkc * 512 + (cl - 1) * 128: dkc * 512 + (cl + 1) * 128],
                                        start=first, stop=False)),
                                    waits=wl if first else ())
                        return p.emit(
                            "pe",
                            (lambda t, sbank=sbank, moff=moff: t.matmul(
                                sbank[:, 0:512], IDN[:], MKs[:, moff:moff + 512],
                                start=False, stop=True)),
                            inc=("pe", 1))

                    smi = {0: s_group(0), 1: s_group(1)}
                    rc = {}
                    pn_done = {}
                    ptv = {}
                    ov = {}

                    def emit_exps(k):
                        b = k % 2
                        sbank = (pA, pB)[b]
                        for hh in range(2):
                            sc_rel[b] = p.emit(
                                "act",
                                (lambda s, sbank=sbank, hh=hh, k=k: s.activation(
                                    PEx[:, ((k % 2) * 2 + hh) * 256:((k % 2) * 2 + hh + 1) * 256],
                                    sbank[:, hh * 256: hh * 256 + 256],
                                    AF.Exp, bias=CST[:, 1:2], scale=0.125,
                                    accum_out=RS[:, (k % 2) * 2 + hh: (k % 2) * 2 + hh + 1])),
                                waits=[("pe", smi[k]), ("dve", pex_rel[(k % 2) * 2 + hh])],
                                inc=("act", 1))
                        return sc_rel[b]

                    ex_done = {0: emit_exps(0)}
                    rc[0] = p.emit(
                        "dve",
                        lambda v: v.reciprocal(RD[:, 0:2], RS[:, 0:2]),
                        waits=[("act", ex_done[0])], inc=("dve", 1))

                    for g2 in range(10):
                        if g2 + 1 <= 7:
                            ex_done[g2 + 1] = emit_exps(g2 + 1)
                        if g2 <= 7:
                            k = g2
                            for hh in range(2):
                                slot = (k % 2) * 2 + hh
                                pex_rel[slot] = pn_done[k] = p.emit(
                                    "dve",
                                    (lambda v, k=k, hh=hh, slot=slot: v.tensor_scalar(
                                        PMs[:, (k % 2) * 512 + hh * 256: (k % 2) * 512 + (hh + 1) * 256],
                                        PEx[:, slot * 256:(slot + 1) * 256],
                                        RD[:, slot: slot + 1],
                                        None, ALU.mult)),
                                    waits=[("dve", rc[k]), ("pe", t_done2[k % 2])],
                                    inc=("dve", 1))
                        if g2 + 2 <= 7:
                            smi[g2 + 2] = s_group(g2 + 2)
                        if g2 <= 7:
                            k = g2
                            for hh in range(2):
                                for kk in range(2):
                                    wl = [("dve", pn_done[k])]
                                    if hh == 0 and kk == 0:
                                        wl.append(pc_rel2[0])
                                        wl.append(pc_rel2[1])
                                    mi = p.emit(
                                        "pe",
                                        (lambda t, k=k, hh=hh, kk=kk: t.transpose(
                                            pC[:, (k % 2) * 512 + (hh * 2 + kk) * 128:
                                               (k % 2) * 512 + (hh * 2 + kk + 1) * 128],
                                            PMs[:, (k % 2) * 512 + hh * 256 + kk * 128:
                                                (k % 2) * 512 + hh * 256 + (kk + 1) * 128],
                                            IDN[:])),
                                        waits=wl,
                                        inc=("pe", 1) if (hh == 1 and kk == 1) else None)
                            t_done2[k % 2] = mi
                        if 1 <= g2 <= 8:
                            k = g2 - 1
                            for hh in range(2):
                                h = 4 * (k // 2) + (k % 2) + 2 * hh
                                if cl == 0:
                                    lhs_prev = Vc[:, h * 64:(h + 1) * 64]
                                else:
                                    lhs_prev = Vb[:, (cl - 1) * 1024 + h * 64:
                                                  (cl - 1) * 1024 + (h + 1) * 64]
                                wl = [("act", max(af_rel2[0], af_rel2[1])),
                                      ("dve", max(ptv[k], v_done))]
                                if gq0 and k == 0 and hh == 0:
                                    wl.append(("act", lg_rel[0]))
                                oslot = (k % 2) * 2 + hh
                                p.emit(
                                    "pe",
                                    (lambda t, lhs_prev=lhs_prev, k=k, hh=hh, oslot=oslot: t.matmul(
                                        pD[0:64, oslot * 128:(oslot + 1) * 128],
                                        lhs_prev,
                                        PTs[:, (k % 2) * 512 + (hh * 2) * 128:
                                            (k % 2) * 512 + (hh * 2 + 1) * 128],
                                        start=True, stop=False)),
                                    waits=wl if hh == 0 else [("dve", v_done)])
                                ov[k] = p.emit(
                                    "pe",
                                    (lambda t, cl=cl, h=h, k=k, hh=hh, oslot=oslot: t.matmul(
                                        pD[0:64, oslot * 128:(oslot + 1) * 128],
                                        Vb[:, cl * 1024 + h * 64: cl * 1024 + (h + 1) * 64],
                                        PTs[:, (k % 2) * 512 + (hh * 2 + 1) * 128:
                                            (k % 2) * 512 + (hh * 2 + 2) * 128],
                                        start=False, stop=True)),
                                    inc=("pe", 1))
                        if g2 + 1 <= 7:
                            k = g2 + 1
                            rc[k] = p.emit(
                                "dve",
                                (lambda v, k=k: v.reciprocal(
                                    RD[:, (k % 2) * 2: (k % 2) * 2 + 2],
                                    RS[:, (k % 2) * 2: (k % 2) * 2 + 2])),
                                waits=[("act", ex_done[k])], inc=("dve", 1))
                        if g2 <= 7:
                            k = g2
                            ptv[k] = p.emit(
                                "dve",
                                (lambda v, k=k: v.tensor_copy(
                                    PTs[:, (k % 2) * 512:(k % 2 + 1) * 512],
                                    pC[:, (k % 2) * 512:(k % 2 + 1) * 512])),
                                waits=[("pe", t_done2[k % 2])], inc=("dve", 1))
                            pc_rel2[k % 2] = ("dve", ptv[k])
                        if 1 <= g2 <= 8:
                            k = g2 - 1
                            for hh in range(2):
                                h = 4 * (k // 2) + (k % 2) + 2 * hh
                                rowb = (h % 2) * 64
                                dc = h // 2
                                oslot = (k % 2) * 2 + hh
                                af_rel2[k % 2] = p.emit(
                                    "act",
                                    (lambda s, rowb=rowb, dc=dc, oslot=oslot: s.activation(
                                        AFb[rowb:rowb + 64, dc * 128:(dc + 1) * 128],
                                        pD[0:64, oslot * 128:(oslot + 1) * 128], AF.Copy)),
                                    waits=[("pe", ov[k])], inc=("act", 1))
                    af_last = af_rel2[1]
                    if alvl < 5:
                        continue
                    par = ef_par[0]
                    wo_bank = (pEF, pGH)[par]
                    for dc in range(NDC):
                        for half in range(2):
                            wl = ([("wconst", consts_w), ("act", af_last),
                                   ("dve", ef_rel[par])] if (dc == 0 and half == 0) else ())
                            mi = p.emit(
                                "pe",
                                (lambda t, wo_bank=wo_bank, dc=dc, half=half: t.matmul(
                                    wo_bank[:, half * 512:(half + 1) * 512],
                                    AFb[:, dc * 128:(dc + 1) * 128],
                                    WO[:, dc * 1024 + half * 512: dc * 1024 + (half + 1) * 512],
                                    start=(dc == 0), stop=(dc == NDC - 1))),
                                waits=wl,
                                inc=("pe", 1) if (dc == NDC - 1 and half == 1) else None)
                    ef_rel[par] = p.emit(
                        "dve",
                        (lambda v, wo_bank=wo_bank, c=c: v.scalar_tensor_tensor(
                            Y[:, c * 1024:(c + 1) * 1024], wo_bank[:, 0:1024], 1.0,
                            Y[:, c * 1024:(c + 1) * 1024], ALU.mult, ALU.add)),
                        waits=[("pe", mi), ("pool", max(xt_done_p))],
                        inc=("dve", 1))
                    ef_par[0] ^= 1
                pending_carry_pe[0] = p.now("pe")

            act_barrier = p.now("act")
            dve_barrier = p.now("dve")
            if "f" not in phases:
                for j in range(NT):
                    y_done[q][j] = p.now("pool")
            for e in range(NE if "f" in phases else 0):
                for hg in range(NHG):
                    idx = e * NHG + hg
                    fb = idx % 2
                    fw = f_ready[fb]
                    fwsem = "wf%d" % fb
                    hb_done_tt = [0, 0]
                    for tt in range(2):
                        hbb = tt
                        for hc in range(NHC):
                            for dc in range(NDC):
                                wl = ([(fwsem, fw), ln_done,
                                       ("act", max(h1_rel[0], sc_rel[0], act_barrier if (idx == 0 and tt == 0 and hc == 0) else 0))] if dc == 0 else ())
                                mi1 = p.emit(
                                    "pe",
                                    (lambda t, fb=fb, dc=dc, hc=hc, tt=tt: t.matmul(
                                        pA[:, 0:512],
                                        W13[:, fb * 8192 + dc * 1024 + hc * 128:
                                            fb * 8192 + dc * 1024 + (hc + 1) * 128],
                                        XN[:, dc * 1024 + tt * 512: dc * 1024 + (tt + 1) * 512],
                                        start=(dc == 0), stop=(dc == NDC - 1))),
                                    waits=wl, inc=("pe", 1) if dc == NDC - 1 else None)
                            for dc in range(NDC):
                                wl = [("dve", h3_rel[0]),
                                      ("act", sc_rel[1] if (idx == 0 and tt == 0 and hc == 0) else 0)] if dc == 0 else ()
                                mi3 = p.emit(
                                    "pe",
                                    (lambda t, fb=fb, dc=dc, hc=hc, tt=tt: t.matmul(
                                        pB[:, 0:512],
                                        W13[:, fb * 8192 + dc * 1024 + 512 + hc * 128:
                                            fb * 8192 + dc * 1024 + 512 + (hc + 1) * 128],
                                        XN[:, dc * 1024 + tt * 512: dc * 1024 + (tt + 1) * 512],
                                        start=(dc == 0), stop=(dc == NDC - 1))),
                                    waits=wl, inc=("pe", 1) if dc == NDC - 1 else None)
                            sp_ = hc % 2
                            h1_rel[0] = p.emit(
                                "act",
                                (lambda s, sp_=sp_: s.activation(
                                    SIL[:, sp_ * 512:(sp_ + 1) * 512], pA[:, 0:512], AF.Silu)),
                                waits=[("pe", mi1), ("dve", h3_rel[0])], inc=("act", 1))
                            h3_rel[0] = p.emit(
                                "dve",
                                (lambda v, sp_=sp_, hbb=hbb, hc=hc: v.tensor_tensor(
                                    HB[:, hbb * 2048 + hc * 512: hbb * 2048 + (hc + 1) * 512],
                                    SIL[:, sp_ * 512:(sp_ + 1) * 512], pB[:, 0:512], ALU.mult)),
                                waits=[("act", h1_rel[0]), ("pe", mi3)], inc=("dve", 1))
                        hb_done_tt[tt] = h3_rel[0]
                    for tt in range(2):
                        hbb = tt
                        for jj in range(4):
                            j = tt * 4 + jj
                            par = ef_par[0]
                            bank = (pEF, pGH)[par]
                            for hc in range(NHC):
                                for half in range(2):
                                    wl = [("dve", max(hb_done_tt[tt], ef_rel[par]))] if (hc == 0 and half == 0) else ()
                                    mi = p.emit(
                                        "pe",
                                        (lambda t, bank=bank, fb=fb, hbb=hbb, hc=hc, jj=jj, half=half: t.matmul(
                                            bank[:, half * 512:(half + 1) * 512],
                                            HB[:, hbb * 2048 + hc * 512 + jj * 128:
                                               hbb * 2048 + hc * 512 + (jj + 1) * 128],
                                            W2[:, fb * 4096 + hc * 1024 + half * 512:
                                               fb * 4096 + hc * 1024 + (half + 1) * 512],
                                            start=(hc == 0), stop=(hc == NHC - 1))),
                                        waits=wl,
                                        inc=("pe", 1) if (hc == NHC - 1 and half == 1) else None)
                            if e < 2:
                                ef_rel[par] = p.emit(
                                    "dve",
                                    (lambda v, bank=bank, j=j: v.scalar_tensor_tensor(
                                        Y[:, j * 1024:(j + 1) * 1024], bank[:, 0:1024], 1.0,
                                        Y[:, j * 1024:(j + 1) * 1024], ALU.mult, ALU.add)),
                                    waits=[("pe", mi)], inc=("dve", 1))
                            else:
                                ef_rel[par] = p.emit(
                                    "dve",
                                    (lambda v, bank=bank, j=j, e=e: v.scalar_tensor_tensor(
                                        Y[:, j * 1024:(j + 1) * 1024], bank[:, 0:1024],
                                        CMB[:, j * 8 + (e - 2): j * 8 + (e - 1)],
                                        Y[:, j * 1024:(j + 1) * 1024], ALU.mult, ALU.add)),
                                    waits=[("pe", mi)], inc=("dve", 1))
                            if e == NE - 1 and hg == NHG - 1:
                                y_done[q][j] = ef_rel[par]
                            ef_par[0] ^= 1
                    f_last_use[fb] = p.now("pe")
                    nidx = idx + 2
                    if nidx < NE * NHG:
                        ne_, nhg_ = divmod(nidx, NHG)
                        load_f(ne_, nhg_)
                    elif q < NQ - 1:
                        ne_, nhg_ = divmod(nidx - NE * NHG, NHG)
                        load_f(ne_, nhg_)

            sc_rel[0] = max(sc_rel[0], h1_rel[0])
            store_sem = "dve" if "f" in phases else "pool"
            for j in range(NT):
                store_done[q][j] = p.emit(
                    "sp",
                    (lambda e_, q=q, j=j: e_.dma_start(
                        out=Yh[q * 1024 + j * 128: q * 1024 + (j + 1) * 128, :],
                        in_=Y[:, j * 1024:(j + 1) * 1024])),
                    waits=[(store_sem, y_done[q][j])], inc=("stq", 16))

        p.wait_only("sp", [("stq", p.now("stq"))])

        def replay(lst):
            def f(eng):
                waited = {}
                for fn, waits, inc in lst:
                    for (s, v) in waits:
                        if waited.get(s, 0) < v:
                            eng.wait_ge(sems[s], int(v))
                            waited[s] = v
                    if fn is None:
                        continue
                    ins = fn(eng)
                    if inc is not None and ins is not None:
                        ins.then_inc(sems[inc[0]], inc[1])
            return f

        block = E(nc.Block())
        block.tensor(replay(p.ops["pe"]))
        block.scalar(replay(p.ops["act"]))
        block.vector(replay(p.ops["dve"]))
        block.gpsimd(replay(p.ops["pool"]))
        block.sync(replay(p.ops["sp"]))

    return nc



def _prep_weights(ins):
    f32 = np.float32

    def wt_blocks(WT):
        return np.ascontiguousarray(
            WT.reshape(8, 128, 1024).transpose(1, 0, 2).reshape(128, 8192))

    WAarr = np.ascontiguousarray(np.stack([
        wt_blocks(np.ascontiguousarray(ins["L0_attn_Wq"].T)),
        wt_blocks(np.ascontiguousarray(ins["L0_attn_Wk"].T)),
        wt_blocks(np.ascontiguousarray(ins["L0_attn_Wv"].T)),
    ])).astype(f32)
    WOarr = wt_blocks(np.ascontiguousarray(ins["L0_attn_Wo"].T)).astype(f32)

    w1 = np.concatenate([ins["L0_ffn_sw1"], ins["L0_ffn_w1"]], axis=0)
    w3 = np.concatenate([ins["L0_ffn_sw3"], ins["L0_ffn_w3"]], axis=0)
    w2 = np.concatenate([ins["L0_ffn_sw2"], ins["L0_ffn_w2"]], axis=0)

    w1T = np.transpose(w1, (0, 2, 1)).reshape(NE, 8, 128, 4, 512)
    w3T = np.transpose(w3, (0, 2, 1)).reshape(NE, 8, 128, 4, 512)
    a1 = np.transpose(w1T, (0, 3, 2, 1, 4))
    a3 = np.transpose(w3T, (0, 3, 2, 1, 4))
    W13arr = np.ascontiguousarray(
        np.concatenate([a1, a3], axis=4).reshape(NE, NHG, 128, 8192)).astype(f32)

    w2T = np.transpose(w2, (0, 2, 1)).reshape(NE, 4, 4, 128, 1024)
    W2arr = np.ascontiguousarray(
        np.transpose(w2T, (0, 1, 3, 2, 4)).reshape(NE, NHG, 128, 4096)).astype(f32)

    rwT = np.ascontiguousarray(ins["L0_ffn_router_w"].T).reshape(8, 128, 8)
    RWarr = np.ascontiguousarray(rwT.transpose(1, 0, 2).reshape(128, 64)).astype(f32)

    BBarr = np.ascontiguousarray(
        np.tile(np.asarray(ins["L0_ffn_expert_bias"]).reshape(1, 8), (128, 1))).astype(f32)

    i = np.arange(128)[:, None]
    j = np.arange(256)[None, :]
    band = ((j >= i + 1) & (j <= i + 128))
    first = band & (j >= 128)
    addm = lambda m: np.where(m, 0.0, -1e30).astype(f32)
    MKarr = np.ascontiguousarray(np.concatenate(
        [addm(band), addm(band), addm(first), addm(first)], axis=1)).astype(f32)

    io = np.tile(np.arange(8, dtype=f32).reshape(1, 8), (128, 1))
    IOarr = np.ascontiguousarray(np.concatenate([io, 8.0 - io], axis=1)).astype(f32)

    IDarr = np.eye(128, dtype=f32)
    CSarr = np.ascontiguousarray(
        np.tile(np.asarray([[1e-5, EXP_SHIFT]], dtype=f32), (128, 1)))

    return dict(wa=WAarr, wo=WOarr, wf13=W13arr, wf2=W2arr, rwt=RWarr,
                biasb=BBarr, masks=MKarr, iotas=IOarr, ident=IDarr, csts=CSarr)


_NC_CACHE = {}


def kernel(**inputs):
    x = np.asarray(inputs["L0_attn_x"], dtype=np.float32)
    shared = _prep_weights({k: np.asarray(v) for k, v in inputs.items()})

    if "main" not in _NC_CACHE:
        _NC_CACHE["main"] = build_nc()
    nc = _NC_CACHE["main"]

    in_maps = []
    for c in range(8):
        m = dict(shared)
        m["x"] = np.ascontiguousarray(x[c])
        in_maps.append(m)

    res = run_bass_kernel_spmd(nc, in_maps, core_ids=list(range(8)))
    out = np.stack([res.results[c]["y"] for c in range(8)], axis=0)
    return out.astype(np.float32)


if __name__ == "__main__":
    build_nc()
    print("built ok")


# version 04
# speedup vs baseline: 1.0978x; 1.0815x over previous
import numpy as np
from contextlib import ExitStack

import concourse.bass as bass
import concourse.mybir as mybir
from concourse.bass_utils import run_bass_kernel_spmd

F32 = mybir.dt.float32
BF16 = mybir.dt.bfloat16
AF = mybir.ActivationFunctionType
ALU = mybir.AluOpType
AX = mybir.AxisListType

SEQ, D = 4096, 1024
NQ, TQ, NT = 4, 1024, 8
NG = 2
NE, NHG, NHC = 10, 4, 4
NDC = 8
EXP_SHIFT = -2.0
NEGBIG = -1e30

ENGINES = ("pe", "act", "dve", "pool", "sp")


class Prog:

    def __init__(self):
        self.ops = {e: [] for e in ENGINES}
        self.cnt = {}

    def emit(self, eng, fn, waits=(), inc=None):
        waits = list(waits)
        if eng in ("act", "dve"):
            if inc is None:
                inc = (eng, 1)
            prev = self.cnt.get(eng, 0)
            if prev:
                waits.append((eng, prev))
        elif eng == "pool" and inc is not None and inc[0] == "pool":
            prev = self.cnt.get("pool", 0)
            if prev:
                waits.append(("pool", prev))
        val = None
        if inc is not None:
            s, a = inc
            self.cnt[s] = self.cnt.get(s, 0) + a
            val = self.cnt[s]
        self.ops[eng].append((fn, [(s, v) for (s, v) in waits if v > 0], inc))
        return val

    def wait_only(self, eng, waits):
        self.ops[eng].append((None, [(s, v) for (s, v) in waits if v > 0], None))

    def now(self, sem):
        return self.cnt.get(sem, 0)


def build_nc(phases="lraf"):
    alvl = 5
    import re as _re
    _m = _re.search(r"a(\d)", phases)
    if _m:
        alvl = int(_m.group(1))
    nc = bass.Bass("TRN2", target_bir_lowering=False)

    Xh = nc.dram_tensor("x", [SEQ, D], F32, kind="ExternalInput").ap()
    WAh = nc.dram_tensor("wa", [3, 128, 8192], F32, kind="ExternalInput").ap()
    WOh = nc.dram_tensor("wo", [128, 8192], F32, kind="ExternalInput").ap()
    W13h = nc.dram_tensor("wf13", [NE, NHG, 128, 8192], F32, kind="ExternalInput").ap()
    W2h = nc.dram_tensor("wf2", [NE, NHG, 128, 4096], F32, kind="ExternalInput").ap()
    RWh = nc.dram_tensor("rwt", [128, 64], F32, kind="ExternalInput").ap()
    BBh = nc.dram_tensor("biasb", [128, 8], F32, kind="ExternalInput").ap()
    MKh = nc.dram_tensor("masks", [128, 1024], F32, kind="ExternalInput").ap()
    IOh = nc.dram_tensor("iotas", [128, 16], F32, kind="ExternalInput").ap()
    IDh = nc.dram_tensor("ident", [128, 128], F32, kind="ExternalInput").ap()
    CSh = nc.dram_tensor("csts", [128, 2], F32, kind="ExternalInput").ap()
    Yh = nc.dram_tensor("y", [SEQ, D], F32, kind="ExternalOutput").ap()

    p = Prog()

    with ExitStack() as ctx:
        E = ctx.enter_context
        sb = lambda name, cols, dt: E(nc.sbuf_tensor(name, [128, cols], dt))

        XN = sb("XN", NDC * 1024, BF16)
        Y = sb("Y", NT * 1024, F32)
        W13 = sb("W13", 2 * 8192, BF16)
        W2 = sb("W2", 2 * 4096, BF16)
        WA = sb("WA", 2 * 8192, BF16)
        WO = sb("WO", 8192, BF16)
        RW = sb("RW", 64, BF16)
        Qb = sb("Qb", NDC * 512, BF16)
        Kb = sb("Kb", NDC * 512, BF16)
        Kc = sb("Kc", NDC * 128, BF16)
        Vb = sb("Vb", 4 * 1024, BF16)
        Vc = sb("Vc", 1024, BF16)
        AFb = sb("AFb", NDC * 128, BF16)
        PEx = sb("PEx", 4 * 256, F32)
        PTs = sb("PTs", 8 * 128, BF16)
        PMs = sb("PMs", 4 * 256, BF16)
        RS = sb("RS", 8, F32)
        RD = sb("RD", 8, F32)
        HB = sb("HB", 2 * 2048, BF16)
        SIL = sb("SIL", 2 * 512, F32)
        XT = sb("XT", 2 * 1024, F32)
        ST = sb("ST", 56, F32)
        RTR = sb("RTR", 64, F32)
        CMB = sb("CMB", NT * 8, F32)
        IDN = sb("IDN", 128, BF16)
        MKs = sb("MKs", 1024, BF16)
        IOs = sb("IOs", 16, F32)
        BBs = sb("BBs", 8, F32)
        CST = sb("CST", 2, F32)

        XTOK = (PMs, PTs)

        pA = E(nc.psum_tensor("pA", [128, 512], F32))
        pB = E(nc.psum_tensor("pB", [128, 512], F32))
        pC = E(nc.psum_tensor("pC", [128, 1024], BF16))
        pD = E(nc.psum_tensor("pD", [128, 512], F32))
        pEF = E(nc.psum_tensor("pEF", [128, 1024], F32))
        pGH = E(nc.psum_tensor("pGH", [128, 1024], F32))

        sems = {n: E(nc.semaphore(n)) for n in
                ("wconst", "wa0", "wa1", "wf0", "wf1", "xd0", "xd1",
                 "xa0", "xa1", "stq", "pe", "act", "dve", "pool")}

        pc_rel = [0]
        sc_rel = {0: 0, 1: 0}
        h1_rel = [0]
        h3_rel = [0]
        ef_rel = {0: 0, 1: 0}
        ef_par = [0]
        pex_rel = {0: 0, 1: 0, 2: 0, 3: 0}
        lg_rel = [0]
        af_rel2 = {0: 0, 1: 0}
        t_done2 = {0: 0, 1: 0}
        pc_rel2 = {0: ("dve", 0), 1: ("dve", 0)}

        p.emit("pool", lambda g: g.dma_start(out=IDN[:], in_=IDh), inc=("wconst", 16))
        p.emit("pool", lambda g: g.dma_start(out=MKs[:], in_=MKh), inc=("wconst", 16))
        p.emit("pool", lambda g: g.dma_start(out=IOs[:], in_=IOh), inc=("wconst", 16))
        p.emit("pool", lambda g: g.dma_start(out=BBs[:], in_=BBh), inc=("wconst", 16))
        p.emit("pool", lambda g: g.dma_start(out=RW[:], in_=RWh), inc=("wconst", 16))
        p.emit("pool", lambda g: g.dma_start(out=WO[:], in_=WOh), inc=("wconst", 16))
        p.emit("pool", lambda g: g.dma_start(out=CST[:], in_=CSh), inc=("wconst", 16))
        consts_w = p.now("wconst")
        p.emit("dve", lambda v: v.memset(Kc[:], 0), inc=("dve", 1))
        p.emit("dve", lambda v: v.memset(Vc[:], 0), inc=("dve", 1))

        wa_seq = [0]
        wa_last_use = [0, 0]
        wa_ready = [0, 0]

        def load_wa(widx):
            b = wa_seq[0] % 2
            wv = p.emit("pool",
                        (lambda g, b=b, widx=widx: g.dma_start(
                            out=WA[:, b * 8192:(b + 1) * 8192], in_=WAh[widx])),
                        waits=[("pe", wa_last_use[b])], inc=("wa%d" % b, 16))
            wa_ready[b] = wv
            wa_seq[0] += 1
            return b

        f_seq = [0]
        f_last_use = [0, 0]
        f_ready = [0, 0]

        def load_f(e, hg):
            b = f_seq[0] % 2
            p.emit("pool",
                   (lambda g, b=b, e=e, hg=hg: g.dma_start(
                       out=W13[:, b * 8192:(b + 1) * 8192], in_=W13h[e, hg])),
                   waits=[("pe", f_last_use[b])], inc=("wf%d" % b, 16))
            wv = p.emit("pool",
                        (lambda g, b=b, e=e, hg=hg: g.dma_start(
                            out=W2[:, b * 4096:(b + 1) * 4096], in_=W2h[e, hg])),
                        inc=("wf%d" % b, 16))
            f_ready[b] = wv
            f_seq[0] += 1

        store_done = np.zeros((NQ, NT), dtype=np.int64)
        pending_carry_pe = [None]
        kcar_done = [0]
        vcar_done = [0]
        y_done = np.zeros((NQ, NT), dtype=np.int64)

        ln_state = {"xt_done": [0, 0], "xt_done_p": [0, 0]}

        def emit_ln_passA(q):
            xa_done = [0, 0]
            xd_done = [0, 0]
            xload = [0, 0]
            xt_done = ln_state["xt_done"]
            xt_done_p = ln_state["xt_done_p"]
            for j in range(NT):
                par = j % 2
                xload[par] = p.emit(
                    "sp",
                    (lambda e, q=q, j=j, par=par: e.dma_start(
                        out=XT[:, par * 1024:(par + 1) * 1024],
                        in_=Xh[q * 1024 + j * 128: q * 1024 + (j + 1) * 128, :])),
                    waits=[("act", xa_done[par]),
                           ("dve", max(xd_done[par], xt_done[par], xt_done_p[par]))],
                    inc=("xa%d" % par, 16))
                p.emit(
                    "act",
                    (lambda s, j=j, par=par: s.activation(
                        pD[:, 0:512], XT[:, par * 1024:par * 1024 + 512],
                        AF.Square, accum_out=ST[:, 8 + j:9 + j])),
                    waits=[("xa%d" % par, xload[par])], inc=("act", 1))
                xa_done[par] = p.emit(
                    "act",
                    (lambda s, j=j, par=par: s.activation(
                        pD[:, 0:512], XT[:, par * 1024 + 512:(par + 1) * 1024],
                        AF.Square, accum_out=ST[:, 48 + j:49 + j])),
                    inc=("act", 1))
                xd_done[par] = p.emit(
                    "dve",
                    (lambda v, j=j, par=par: v.reduce_sum(
                        ST[:, j:j + 1], XT[:, par * 1024:(par + 1) * 1024], AX.X)),
                    waits=[("xa%d" % par, xload[par])], inc=("dve", 1))
            p.emit("dve", lambda v: v.tensor_scalar(
                ST[:, 16:24], ST[:, 0:8], 1.0 / D, None, ALU.mult),
                waits=[("act", xa_done[0]), ("act", xa_done[1])], inc=("dve", 1))
            p.emit("dve", lambda v: v.tensor_tensor(
                ST[:, 24:32], ST[:, 16:24], ST[:, 16:24], ALU.mult), inc=("dve", 1))
            p.emit("dve", lambda v: v.tensor_tensor(
                ST[:, 32:40], ST[:, 8:16], ST[:, 48:56], ALU.add), inc=("dve", 1))
            p.emit("dve", lambda v: v.tensor_scalar(
                ST[:, 32:40], ST[:, 32:40], 1.0 / D, None, ALU.mult), inc=("dve", 1))
            stats_dve = p.emit("dve", lambda v: v.tensor_tensor(
                ST[:, 32:40], ST[:, 32:40], ST[:, 24:32], ALU.subtract), inc=("dve", 1))
            p.emit("act", lambda s: s.activation(
                ST[:, 40:48], ST[:, 32:40], AF.Ln, bias=CST[:, 0:1], scale=1.0),
                waits=[("dve", stats_dve), ("wconst", consts_w)], inc=("act", 1))
            rstd_act = p.emit("act", lambda s: s.activation(
                ST[:, 40:48], ST[:, 40:48], AF.Exp, bias=0.0, scale=-0.5),
                inc=("act", 1))
            ln_state[("xa", q)] = xa_done
            ln_state[("xd", q)] = xd_done
            ln_state[("xload", q)] = xload
            ln_state[("rstd", q)] = rstd_act

        emit_ln_passA(0)
        for q in range(NQ):
            if ("xa", q) not in ln_state:
                emit_ln_passA(q)
            xa_done = ln_state.pop(("xa", q))
            xd_done = ln_state.pop(("xd", q))
            xload = ln_state.pop(("xload", q))
            rstd_act = ln_state.pop(("rstd", q))
            xt_done = [xd_done[0], xd_done[1]]
            xt_done_p = [0, 0]
            ln_state["xt_done"] = xt_done
            ln_state["xt_done_p"] = xt_done_p
            tok_done = [0, 0]
            for j in range(NT):
                par = j % 2
                xload[par] = p.emit(
                    "pool",
                    (lambda e, q=q, j=j, par=par: e.dma_start(
                        out=XT[:, par * 1024:(par + 1) * 1024],
                        in_=Xh[q * 1024 + j * 128: q * 1024 + (j + 1) * 128, :])),
                    waits=[("dve", max(xt_done[par], xt_done_p[par])),
                           ("act", xa_done[par])],
                    inc=("xd%d" % par, 16))
                apply_v = p.emit(
                    "dve",
                    (lambda v, j=j, par=par: v.tensor_scalar(
                        XTOK[par][:],
                        XT[:, par * 1024:(par + 1) * 1024],
                        ST[:, 16 + j:17 + j], ST[:, 40 + j:41 + j],
                        ALU.subtract, ALU.mult)),
                    waits=[("xd%d" % par, xload[par]), ("act", rstd_act),
                           ("pe", tok_done[par])],
                    inc=("dve", 1))
                xt_done[par] = apply_v
                xt_done_p[par] = p.emit(
                    "dve",
                    (lambda v, j=j, par=par: v.tensor_copy(
                        Y[:, j * 1024:(j + 1) * 1024],
                        XT[:, par * 1024:(par + 1) * 1024])),
                    waits=[("xd%d" % par, xload[par]),
                           ("stq", 128 * q)],
                    inc=("dve", 1))
                for dc in range(NDC):
                    wl = [("dve", apply_v)]
                    if dc == 0:
                        m0, m1 = pc_rel2[0], pc_rel2[1]
                        wl += [m0, m1, ("wconst", consts_w)]
                    mi = p.emit(
                        "pe",
                        (lambda t, dc=dc, par=par: t.transpose(
                            pC[:, dc * 128:(dc + 1) * 128],
                            XTOK[par][:, dc * 128:(dc + 1) * 128],
                            IDN[:])),
                        waits=wl, inc=("pe", 1) if dc == NDC - 1 else None)
                tok_done[par] = mi
                xn_ev = p.emit(
                    "dve",
                    (lambda v, j=j: v.tensor_copy(
                        XN.rearrange("p (dc t) -> p dc t", dc=NDC)[:, :, j * 128:(j + 1) * 128],
                        pC.rearrange("p (dc t) -> p dc t", dc=NDC))),
                    waits=[("pe", tok_done[par])], inc=("dve", 1))
                pc_rel2[0] = pc_rel2[1] = ("dve", xn_ev)
                if j == 3:
                    xn_half = xn_ev
            ln_done = ("dve", max(pc_rel2[0][1], pc_rel2[1][1]))
            ln_done_g = [("dve", xn_half), ln_done]

            def emit_router_tile(j):
                for dc in range(NDC):
                    wl = [ln_done, ("act", lg_rel[0])] if dc == 0 else ()
                    mms = p.emit(
                        "pe",
                        (lambda t, j=j, dc=dc: t.matmul(
                            pD[:, 0:8],
                            XN[:, dc * 1024 + j * 128: dc * 1024 + (j + 1) * 128],
                            RW[:, dc * 8:(dc + 1) * 8],
                            start=(dc == 0), stop=(dc == NDC - 1))),
                        waits=wl, inc=("pe", 1) if dc == NDC - 1 else None)
                lg_rel[0] = p.emit(
                    "act",
                    lambda s: s.activation(RTR[:, 0:8], pD[:, 0:8], AF.Copy),
                    waits=[("pe", mms)], inc=("act", 1))
                p.emit("dve", lambda v: v.tensor_tensor(
                    RTR[:, 0:8], RTR[:, 0:8], BBs[:], ALU.add),
                    waits=[("act", lg_rel[0])], inc=("dve", 1))
                p.emit("dve", lambda v: v.reduce_max(
                    RTR[:, 8:9], RTR[:, 0:8], AX.X), inc=("dve", 1))
                p.emit("dve", lambda v: v.tensor_scalar(
                    RTR[:, 16:24], RTR[:, 0:8], RTR[:, 8:9], None, ALU.is_equal), inc=("dve", 1))
                p.emit("dve", lambda v: v.scalar_tensor_tensor(
                    RTR[:, 32:40], RTR[:, 16:24], NEGBIG, RTR[:, 0:8], ALU.mult, ALU.add), inc=("dve", 1))
                p.emit("dve", lambda v: v.reduce_max(
                    RTR[:, 9:10], RTR[:, 32:40], AX.X), inc=("dve", 1))
                p.emit("dve", lambda v: v.tensor_scalar(
                    RTR[:, 24:32], RTR[:, 32:40], RTR[:, 9:10], None, ALU.is_equal), inc=("dve", 1))
                p.emit("dve", lambda v: v.tensor_tensor(
                    RTR[:, 16:24], RTR[:, 16:24], RTR[:, 24:32], ALU.add), inc=("dve", 1))
                sm_v = p.emit("dve", lambda v: v.tensor_tensor(
                    RTR[:, 10:11], RTR[:, 8:9], RTR[:, 9:10], ALU.add), inc=("dve", 1))
                e_v = p.emit("act", lambda s: s.activation(
                    RTR[:, 40:48], RTR[:, 0:8], AF.Exp, bias=RTR[:, 10:11], scale=-2.0),
                    waits=[("dve", sm_v)], inc=("act", 1))
                p.emit("dve", lambda v: v.tensor_scalar(
                    RTR[:, 48:56], RTR[:, 40:48], 1.0, None, ALU.add),
                    waits=[("act", e_v)], inc=("dve", 1))
                rec_v = p.emit("dve", lambda v: v.reciprocal(
                    RTR[:, 40:48], RTR[:, 48:56]), inc=("dve", 1))
                p.emit("dve", (lambda v, j=j: v.tensor_tensor(
                    CMB[:, j * 8:(j + 1) * 8], RTR[:, 40:48], RTR[:, 16:24], ALU.mult)),
                    waits=[("dve", rec_v)], inc=("dve", 1))

            for g in range(NG if "a" in phases else 0):
                bq = load_wa(0)
                bk = load_wa(1)
                if pending_carry_pe[0] is not None:
                    pe_now = pending_carry_pe[0]
                    for dkc in range(NDC):
                        kcar_done[0] = p.emit(
                            "pool",
                            (lambda g_, dkc=dkc: g_.tensor_copy(
                                Kc[:, dkc * 128:(dkc + 1) * 128],
                                Kb[:, dkc * 512 + 384: dkc * 512 + 512])),
                            waits=[("pe", pe_now)], inc=("pool", 1))
                    vcar_done[0] = p.emit(
                        "pool",
                        lambda g_: g_.tensor_copy(Vc[:], Vb[:, 3 * 1024:4 * 1024]),
                        inc=("pool", 1))
                    pending_carry_pe[0] = None
                for dqc in range(NDC):
                    par = ef_par[0]
                    bank = (pEF, pGH)[par]
                    for dc in range(NDC):
                        wl = [("wa%d" % bq, wa_ready[bq]), ln_done_g[g],
                              ("dve", ef_rel[par])] if dc == 0 else ()
                        mi = p.emit(
                            "pe",
                            (lambda t, bank=bank, dqc=dqc, dc=dc, g=g, bq=bq: t.matmul(
                                bank[:, 0:512],
                                WA[:, bq * 8192 + dc * 1024 + dqc * 128:
                                   bq * 8192 + dc * 1024 + (dqc + 1) * 128],
                                XN[:, dc * 1024 + g * 512: dc * 1024 + (g + 1) * 512],
                                start=(dc == 0), stop=(dc == NDC - 1))),
                            waits=wl, inc=("pe", 1) if dc == NDC - 1 else None)
                    ef_rel[par] = p.emit(
                        "dve",
                        (lambda v, bank=bank, dqc=dqc: v.tensor_copy(
                            Qb[:, dqc * 512:(dqc + 1) * 512], bank[:, 0:512])),
                        waits=[("pe", mi)], inc=("dve", 1))
                    ef_par[0] ^= 1
                    if g == 0 and "r" in phases:
                        emit_router_tile(dqc)
                wa_last_use[bq] = p.now("pe")
                q_done = p.now("dve")
                for dkc in range(NDC):
                    par = ef_par[0]
                    bank = (pEF, pGH)[par]
                    for dc in range(NDC):
                        wl = [("wa%d" % bk, wa_ready[bk]), ("dve", ef_rel[par]),
                              ("pool", kcar_done[0])] if dc == 0 else ()
                        mi = p.emit(
                            "pe",
                            (lambda t, bank=bank, dkc=dkc, dc=dc, g=g, bk=bk: t.matmul(
                                bank[:, 0:512],
                                WA[:, bk * 8192 + dc * 1024 + dkc * 128:
                                   bk * 8192 + dc * 1024 + (dkc + 1) * 128],
                                XN[:, dc * 1024 + g * 512: dc * 1024 + (g + 1) * 512],
                                start=(dc == 0), stop=(dc == NDC - 1))),
                            waits=wl, inc=("pe", 1) if dc == NDC - 1 else None)
                    ef_rel[par] = p.emit(
                        "dve",
                        (lambda v, bank=bank, dkc=dkc: v.tensor_copy(
                            Kb[:, dkc * 512:(dkc + 1) * 512], bank[:, 0:512])),
                        waits=[("pe", mi)], inc=("dve", 1))
                    ef_par[0] ^= 1
                bv = load_wa(2)
                wa_last_use[bk] = p.now("pe")
                k_done = p.now("dve")
                for tl in range(4):
                    par = ef_par[0]
                    bank = (pEF, pGH)[par]
                    for half in range(2):
                        for dc in range(NDC):
                            wl = [("wa%d" % bv, wa_ready[bv]), ("dve", ef_rel[par]),
                                  ("pool", vcar_done[0])] if (dc == 0 and half == 0) else ()
                            mi = p.emit(
                                "pe",
                                (lambda t, bank=bank, tl=tl, dc=dc, g=g, bv=bv, half=half: t.matmul(
                                    bank[:, half * 512:(half + 1) * 512],
                                    XN[:, dc * 1024 + g * 512 + tl * 128:
                                       dc * 1024 + g * 512 + (tl + 1) * 128],
                                    WA[:, bv * 8192 + dc * 1024 + half * 512:
                                       bv * 8192 + dc * 1024 + (half + 1) * 512],
                                    start=(dc == 0), stop=(dc == NDC - 1))),
                                waits=wl,
                                inc=("pe", 1) if (dc == NDC - 1 and half == 1) else None)
                    ef_rel[par] = p.emit(
                        "dve",
                        (lambda v, bank=bank, tl=tl: v.tensor_copy(
                            Vb[:, tl * 1024:(tl + 1) * 1024], bank[:, 0:1024])),
                        waits=[("pe", mi)], inc=("dve", 1))
                    ef_par[0] ^= 1
                wa_last_use[bv] = p.now("pe")
                v_done = p.now("dve")

                if q == 0 and g == 0:
                    load_f(0, 0)
                    load_f(0, 1)

                for cl in range(4 if alvl >= 2 else 0):
                    c = g * 4 + cl
                    gq0 = (q == 0 and c == 0)
                    moff = 512 if gq0 else 0

                    def s_group(g2, cl=cl, moff=moff, k_done=k_done):
                        b = g2 % 2
                        sbank = (pA, pB)[b]
                        wl = [("dve", k_done), ("act", sc_rel[b])]
                        for hh in range(2):
                            h = 4 * (g2 // 2) + (g2 % 2) + 2 * hh
                            rowb = (g2 % 2) * 64
                            dkc = h // 2
                            first = (hh == 0)
                            if cl == 0:
                                p.emit(
                                    "pe",
                                    (lambda t, sbank=sbank, hh=hh, rowb=rowb, dkc=dkc, cl=cl, first=first: t.matmul(
                                        sbank[:, hh * 256: hh * 256 + 128],
                                        Qb[rowb:rowb + 64, dkc * 512 + cl * 128: dkc * 512 + (cl + 1) * 128],
                                        Kc[rowb:rowb + 64, dkc * 128:(dkc + 1) * 128],
                                        start=first, stop=False)),
                                    waits=wl if first else ())
                                p.emit(
                                    "pe",
                                    (lambda t, sbank=sbank, hh=hh, rowb=rowb, dkc=dkc, cl=cl: t.matmul(
                                        sbank[:, hh * 256 + 128: hh * 256 + 256],
                                        Qb[rowb:rowb + 64, dkc * 512 + cl * 128: dkc * 512 + (cl + 1) * 128],
                                        Kb[rowb:rowb + 64, dkc * 512: dkc * 512 + 128],
                                        start=False, stop=False)))
                            else:
                                p.emit(
                                    "pe",
                                    (lambda t, sbank=sbank, hh=hh, rowb=rowb, dkc=dkc, cl=cl, first=first: t.matmul(
                                        sbank[:, hh * 256: hh * 256 + 256],
                                        Qb[rowb:rowb + 64, dkc * 512 + cl * 128: dkc * 512 + (cl + 1) * 128],
                                        Kb[rowb:rowb + 64, dkc * 512 + (cl - 1) * 128: dkc * 512 + (cl + 1) * 128],
                                        start=first, stop=False)),
                                    waits=wl if first else ())
                        return p.emit(
                            "pe",
                            (lambda t, sbank=sbank, moff=moff: t.matmul(
                                sbank[:, 0:512], IDN[:], MKs[:, moff:moff + 512],
                                start=False, stop=True)),
                            inc=("pe", 1))

                    smi = {0: s_group(0), 1: s_group(1)}
                    rc = {}
                    pn_done = {}
                    ptv = {}
                    ov = {}

                    def emit_exps(k):
                        b = k % 2
                        sbank = (pA, pB)[b]
                        for hh in range(2):
                            sc_rel[b] = p.emit(
                                "act",
                                (lambda s, sbank=sbank, hh=hh, k=k: s.activation(
                                    PEx[:, ((k % 2) * 2 + hh) * 256:((k % 2) * 2 + hh + 1) * 256],
                                    sbank[:, hh * 256: hh * 256 + 256],
                                    AF.Exp, bias=CST[:, 1:2], scale=0.125,
                                    accum_out=RS[:, (k % 2) * 2 + hh: (k % 2) * 2 + hh + 1])),
                                waits=[("pe", smi[k]), ("dve", pex_rel[(k % 2) * 2 + hh])],
                                inc=("act", 1))
                        return sc_rel[b]

                    ex_done = {0: emit_exps(0)}
                    rc[0] = p.emit(
                        "dve",
                        lambda v: v.reciprocal(RD[:, 0:2], RS[:, 0:2]),
                        waits=[("act", ex_done[0])], inc=("dve", 1))

                    for g2 in range(10):
                        if g2 + 1 <= 7:
                            ex_done[g2 + 1] = emit_exps(g2 + 1)
                        if g2 <= 7:
                            k = g2
                            for hh in range(2):
                                slot = (k % 2) * 2 + hh
                                pex_rel[slot] = pn_done[k] = p.emit(
                                    "dve",
                                    (lambda v, k=k, hh=hh, slot=slot: v.tensor_scalar(
                                        PMs[:, (k % 2) * 512 + hh * 256: (k % 2) * 512 + (hh + 1) * 256],
                                        PEx[:, slot * 256:(slot + 1) * 256],
                                        RD[:, slot: slot + 1],
                                        None, ALU.mult)),
                                    waits=[("dve", rc[k]), ("pe", t_done2[k % 2])],
                                    inc=("dve", 1))
                        if g2 + 2 <= 7:
                            smi[g2 + 2] = s_group(g2 + 2)
                        if g2 <= 7:
                            k = g2
                            for hh in range(2):
                                for kk in range(2):
                                    wl = [("dve", pn_done[k])]
                                    if hh == 0 and kk == 0:
                                        wl.append(pc_rel2[0])
                                        wl.append(pc_rel2[1])
                                    mi = p.emit(
                                        "pe",
                                        (lambda t, k=k, hh=hh, kk=kk: t.transpose(
                                            pC[:, (k % 2) * 512 + (hh * 2 + kk) * 128:
                                               (k % 2) * 512 + (hh * 2 + kk + 1) * 128],
                                            PMs[:, (k % 2) * 512 + hh * 256 + kk * 128:
                                                (k % 2) * 512 + hh * 256 + (kk + 1) * 128],
                                            IDN[:])),
                                        waits=wl,
                                        inc=("pe", 1) if (hh == 1 and kk == 1) else None)
                            t_done2[k % 2] = mi
                        if 1 <= g2 <= 8:
                            k = g2 - 1
                            for hh in range(2):
                                h = 4 * (k // 2) + (k % 2) + 2 * hh
                                if cl == 0:
                                    lhs_prev = Vc[:, h * 64:(h + 1) * 64]
                                else:
                                    lhs_prev = Vb[:, (cl - 1) * 1024 + h * 64:
                                                  (cl - 1) * 1024 + (h + 1) * 64]
                                wl = [("act", max(af_rel2[0], af_rel2[1])),
                                      ("dve", max(ptv[k], v_done))]
                                if c == 0 and k == 0 and hh == 0:
                                    wl.append(("act", lg_rel[0]))
                                oslot = (k % 2) * 2 + hh
                                p.emit(
                                    "pe",
                                    (lambda t, lhs_prev=lhs_prev, k=k, hh=hh, oslot=oslot: t.matmul(
                                        pD[0:64, oslot * 128:(oslot + 1) * 128],
                                        lhs_prev,
                                        PTs[:, (k % 2) * 512 + (hh * 2) * 128:
                                            (k % 2) * 512 + (hh * 2 + 1) * 128],
                                        start=True, stop=False)),
                                    waits=wl if hh == 0 else [("dve", v_done)])
                                ov[k] = p.emit(
                                    "pe",
                                    (lambda t, cl=cl, h=h, k=k, hh=hh, oslot=oslot: t.matmul(
                                        pD[0:64, oslot * 128:(oslot + 1) * 128],
                                        Vb[:, cl * 1024 + h * 64: cl * 1024 + (h + 1) * 64],
                                        PTs[:, (k % 2) * 512 + (hh * 2 + 1) * 128:
                                            (k % 2) * 512 + (hh * 2 + 2) * 128],
                                        start=False, stop=True)),
                                    inc=("pe", 1))
                        if g2 + 1 <= 7:
                            k = g2 + 1
                            rc[k] = p.emit(
                                "dve",
                                (lambda v, k=k: v.reciprocal(
                                    RD[:, (k % 2) * 2: (k % 2) * 2 + 2],
                                    RS[:, (k % 2) * 2: (k % 2) * 2 + 2])),
                                waits=[("act", ex_done[k])], inc=("dve", 1))
                        if g2 <= 7:
                            k = g2
                            ptv[k] = p.emit(
                                "dve",
                                (lambda v, k=k: v.tensor_copy(
                                    PTs[:, (k % 2) * 512:(k % 2 + 1) * 512],
                                    pC[:, (k % 2) * 512:(k % 2 + 1) * 512])),
                                waits=[("pe", t_done2[k % 2])], inc=("dve", 1))
                            pc_rel2[k % 2] = ("dve", ptv[k])
                        if 1 <= g2 <= 8:
                            k = g2 - 1
                            for hh in range(2):
                                h = 4 * (k // 2) + (k % 2) + 2 * hh
                                rowb = (h % 2) * 64
                                dc = h // 2
                                oslot = (k % 2) * 2 + hh
                                af_rel2[k % 2] = p.emit(
                                    "act",
                                    (lambda s, rowb=rowb, dc=dc, oslot=oslot: s.activation(
                                        AFb[rowb:rowb + 64, dc * 128:(dc + 1) * 128],
                                        pD[0:64, oslot * 128:(oslot + 1) * 128], AF.Copy)),
                                    waits=[("pe", ov[k])], inc=("act", 1))
                    af_last = af_rel2[1]
                    if alvl < 5:
                        continue
                    par = ef_par[0]
                    wo_bank = (pEF, pGH)[par]
                    for dc in range(NDC):
                        for half in range(2):
                            wl = ([("wconst", consts_w), ("act", af_last),
                                   ("dve", ef_rel[par])] if (dc == 0 and half == 0) else ())
                            mi = p.emit(
                                "pe",
                                (lambda t, wo_bank=wo_bank, dc=dc, half=half: t.matmul(
                                    wo_bank[:, half * 512:(half + 1) * 512],
                                    AFb[:, dc * 128:(dc + 1) * 128],
                                    WO[:, dc * 1024 + half * 512: dc * 1024 + (half + 1) * 512],
                                    start=(dc == 0), stop=(dc == NDC - 1))),
                                waits=wl,
                                inc=("pe", 1) if (dc == NDC - 1 and half == 1) else None)
                    ef_rel[par] = p.emit(
                        "dve",
                        (lambda v, wo_bank=wo_bank, c=c: v.scalar_tensor_tensor(
                            Y[:, c * 1024:(c + 1) * 1024], wo_bank[:, 0:1024], 1.0,
                            Y[:, c * 1024:(c + 1) * 1024], ALU.mult, ALU.add)),
                        waits=[("pe", mi)],
                        inc=("dve", 1))
                    ef_par[0] ^= 1
                pending_carry_pe[0] = p.now("pe")

            act_barrier = p.now("act")
            dve_barrier = p.now("dve")
            if "f" not in phases:
                for j in range(NT):
                    y_done[q][j] = p.now("pool")
            for e in range(NE if "f" in phases else 0):
                for hg in range(NHG):
                    idx = e * NHG + hg
                    fb = idx % 2
                    fw = f_ready[fb]
                    fwsem = "wf%d" % fb
                    hb_done_tt = [0, 0]
                    for tt in range(2):
                        hbb = tt
                        for hc in range(NHC):
                            for dc in range(NDC):
                                wl = ([(fwsem, fw), ln_done,
                                       ("act", max(h1_rel[0], sc_rel[0], act_barrier if (idx == 0 and tt == 0 and hc == 0) else 0))] if dc == 0 else ())
                                mi1 = p.emit(
                                    "pe",
                                    (lambda t, fb=fb, dc=dc, hc=hc, tt=tt: t.matmul(
                                        pA[:, 0:512],
                                        W13[:, fb * 8192 + dc * 1024 + hc * 128:
                                            fb * 8192 + dc * 1024 + (hc + 1) * 128],
                                        XN[:, dc * 1024 + tt * 512: dc * 1024 + (tt + 1) * 512],
                                        start=(dc == 0), stop=(dc == NDC - 1))),
                                    waits=wl, inc=("pe", 1) if dc == NDC - 1 else None)
                            for dc in range(NDC):
                                wl = [("dve", h3_rel[0]),
                                      ("act", sc_rel[1] if (idx == 0 and tt == 0 and hc == 0) else 0)] if dc == 0 else ()
                                mi3 = p.emit(
                                    "pe",
                                    (lambda t, fb=fb, dc=dc, hc=hc, tt=tt: t.matmul(
                                        pB[:, 0:512],
                                        W13[:, fb * 8192 + dc * 1024 + 512 + hc * 128:
                                            fb * 8192 + dc * 1024 + 512 + (hc + 1) * 128],
                                        XN[:, dc * 1024 + tt * 512: dc * 1024 + (tt + 1) * 512],
                                        start=(dc == 0), stop=(dc == NDC - 1))),
                                    waits=wl, inc=("pe", 1) if dc == NDC - 1 else None)
                            sp_ = hc % 2
                            h1_rel[0] = p.emit(
                                "act",
                                (lambda s, sp_=sp_: s.activation(
                                    SIL[:, sp_ * 512:(sp_ + 1) * 512], pA[:, 0:512], AF.Silu)),
                                waits=[("pe", mi1), ("dve", h3_rel[0])], inc=("act", 1))
                            h3_rel[0] = p.emit(
                                "dve",
                                (lambda v, sp_=sp_, hbb=hbb, hc=hc: v.tensor_tensor(
                                    HB[:, hbb * 2048 + hc * 512: hbb * 2048 + (hc + 1) * 512],
                                    SIL[:, sp_ * 512:(sp_ + 1) * 512], pB[:, 0:512], ALU.mult)),
                                waits=[("act", h1_rel[0]), ("pe", mi3)], inc=("dve", 1))
                        hb_done_tt[tt] = h3_rel[0]
                    for tt in range(2):
                        hbb = tt
                        for jj in range(4):
                            j = tt * 4 + jj
                            par = ef_par[0]
                            bank = (pEF, pGH)[par]
                            for hc in range(NHC):
                                for half in range(2):
                                    wl = [("dve", max(hb_done_tt[tt], ef_rel[par]))] if (hc == 0 and half == 0) else ()
                                    mi = p.emit(
                                        "pe",
                                        (lambda t, bank=bank, fb=fb, hbb=hbb, hc=hc, jj=jj, half=half: t.matmul(
                                            bank[:, half * 512:(half + 1) * 512],
                                            HB[:, hbb * 2048 + hc * 512 + jj * 128:
                                               hbb * 2048 + hc * 512 + (jj + 1) * 128],
                                            W2[:, fb * 4096 + hc * 1024 + half * 512:
                                               fb * 4096 + hc * 1024 + (half + 1) * 512],
                                            start=(hc == 0), stop=(hc == NHC - 1))),
                                        waits=wl,
                                        inc=("pe", 1) if (hc == NHC - 1 and half == 1) else None)
                            if e < 2:
                                ef_rel[par] = p.emit(
                                    "dve",
                                    (lambda v, bank=bank, j=j: v.scalar_tensor_tensor(
                                        Y[:, j * 1024:(j + 1) * 1024], bank[:, 0:1024], 1.0,
                                        Y[:, j * 1024:(j + 1) * 1024], ALU.mult, ALU.add)),
                                    waits=[("pe", mi)], inc=("dve", 1))
                            else:
                                ef_rel[par] = p.emit(
                                    "dve",
                                    (lambda v, bank=bank, j=j, e=e: v.scalar_tensor_tensor(
                                        Y[:, j * 1024:(j + 1) * 1024], bank[:, 0:1024],
                                        CMB[:, j * 8 + (e - 2): j * 8 + (e - 1)],
                                        Y[:, j * 1024:(j + 1) * 1024], ALU.mult, ALU.add)),
                                    waits=[("pe", mi)], inc=("dve", 1))
                            if e == NE - 1 and hg == NHG - 1:
                                y_done[q][j] = ef_rel[par]
                            ef_par[0] ^= 1
                    f_last_use[fb] = p.now("pe")
                    if idx == 29 and q + 1 < NQ and "f" in phases:
                        emit_ln_passA(q + 1)
                    nidx = idx + 2
                    if nidx < NE * NHG:
                        ne_, nhg_ = divmod(nidx, NHG)
                        load_f(ne_, nhg_)
                    elif q < NQ - 1:
                        ne_, nhg_ = divmod(nidx - NE * NHG, NHG)
                        load_f(ne_, nhg_)

            sc_rel[0] = max(sc_rel[0], h1_rel[0])
            store_sem = "dve" if "f" in phases else "pool"
            for j in range(NT):
                store_done[q][j] = p.emit(
                    "sp",
                    (lambda e_, q=q, j=j: e_.dma_start(
                        out=Yh[q * 1024 + j * 128: q * 1024 + (j + 1) * 128, :],
                        in_=Y[:, j * 1024:(j + 1) * 1024])),
                    waits=[(store_sem, y_done[q][j])], inc=("stq", 16))

        p.wait_only("sp", [("stq", p.now("stq"))])

        def replay(lst):
            def f(eng):
                waited = {}
                for fn, waits, inc in lst:
                    for (s, v) in waits:
                        if waited.get(s, 0) < v:
                            eng.wait_ge(sems[s], int(v))
                            waited[s] = v
                    if fn is None:
                        continue
                    ins = fn(eng)
                    if inc is not None and ins is not None:
                        ins.then_inc(sems[inc[0]], inc[1])
            return f

        block = E(nc.Block())
        block.tensor(replay(p.ops["pe"]))
        block.scalar(replay(p.ops["act"]))
        block.vector(replay(p.ops["dve"]))
        block.gpsimd(replay(p.ops["pool"]))
        block.sync(replay(p.ops["sp"]))

    return nc



def _prep_weights(ins):
    f32 = np.float32

    def wt_blocks(WT):
        return np.ascontiguousarray(
            WT.reshape(8, 128, 1024).transpose(1, 0, 2).reshape(128, 8192))

    WAarr = np.ascontiguousarray(np.stack([
        wt_blocks(np.ascontiguousarray(ins["L0_attn_Wq"].T)),
        wt_blocks(np.ascontiguousarray(ins["L0_attn_Wk"].T)),
        wt_blocks(np.ascontiguousarray(ins["L0_attn_Wv"].T)),
    ])).astype(f32)
    WOarr = wt_blocks(np.ascontiguousarray(ins["L0_attn_Wo"].T)).astype(f32)

    w1 = np.concatenate([ins["L0_ffn_sw1"], ins["L0_ffn_w1"]], axis=0)
    w3 = np.concatenate([ins["L0_ffn_sw3"], ins["L0_ffn_w3"]], axis=0)
    w2 = np.concatenate([ins["L0_ffn_sw2"], ins["L0_ffn_w2"]], axis=0)

    w1T = np.transpose(w1, (0, 2, 1)).reshape(NE, 8, 128, 4, 512)
    w3T = np.transpose(w3, (0, 2, 1)).reshape(NE, 8, 128, 4, 512)
    a1 = np.transpose(w1T, (0, 3, 2, 1, 4))
    a3 = np.transpose(w3T, (0, 3, 2, 1, 4))
    W13arr = np.ascontiguousarray(
        np.concatenate([a1, a3], axis=4).reshape(NE, NHG, 128, 8192)).astype(f32)

    w2T = np.transpose(w2, (0, 2, 1)).reshape(NE, 4, 4, 128, 1024)
    W2arr = np.ascontiguousarray(
        np.transpose(w2T, (0, 1, 3, 2, 4)).reshape(NE, NHG, 128, 4096)).astype(f32)

    rwT = np.ascontiguousarray(ins["L0_ffn_router_w"].T).reshape(8, 128, 8)
    RWarr = np.ascontiguousarray(rwT.transpose(1, 0, 2).reshape(128, 64)).astype(f32)

    BBarr = np.ascontiguousarray(
        np.tile(np.asarray(ins["L0_ffn_expert_bias"]).reshape(1, 8), (128, 1))).astype(f32)

    i = np.arange(128)[:, None]
    j = np.arange(256)[None, :]
    band = ((j >= i + 1) & (j <= i + 128))
    first = band & (j >= 128)
    addm = lambda m: np.where(m, 0.0, -1e30).astype(f32)
    MKarr = np.ascontiguousarray(np.concatenate(
        [addm(band), addm(band), addm(first), addm(first)], axis=1)).astype(f32)

    io = np.tile(np.arange(8, dtype=f32).reshape(1, 8), (128, 1))
    IOarr = np.ascontiguousarray(np.concatenate([io, 8.0 - io], axis=1)).astype(f32)

    IDarr = np.eye(128, dtype=f32)
    CSarr = np.ascontiguousarray(
        np.tile(np.asarray([[1e-5, EXP_SHIFT]], dtype=f32), (128, 1)))

    return dict(wa=WAarr, wo=WOarr, wf13=W13arr, wf2=W2arr, rwt=RWarr,
                biasb=BBarr, masks=MKarr, iotas=IOarr, ident=IDarr, csts=CSarr)


_NC_CACHE = {}


def kernel(**inputs):
    x = np.asarray(inputs["L0_attn_x"], dtype=np.float32)
    shared = _prep_weights({k: np.asarray(v) for k, v in inputs.items()})

    if "main" not in _NC_CACHE:
        _NC_CACHE["main"] = build_nc()
    nc = _NC_CACHE["main"]

    in_maps = []
    for c in range(8):
        m = dict(shared)
        m["x"] = np.ascontiguousarray(x[c])
        in_maps.append(m)

    res = run_bass_kernel_spmd(nc, in_maps, core_ids=list(range(8)))
    out = np.stack([res.results[c]["y"] for c in range(8)], axis=0)
    return out.astype(np.float32)


if __name__ == "__main__":
    build_nc()
    print("built ok")
